# version 10
# baseline (speedup 1.0000x reference)
import math
from contextlib import ExitStack

import numpy as np
import ml_dtypes
import concourse.bass as bass
import concourse.mybir as mybir
from concourse.bass_utils import run_bass_kernel_spmd
from concourse.bass_core_rust import IndirectOffsetOnAxis
from concourse import library_config as LC

F32 = mybir.dt.float32
BF16 = mybir.dt.bfloat16
I16 = mybir.dt.int16
I32 = mybir.dt.int32
U32 = mybir.dt.uint32
AF = mybir.ActivationFunctionType
ALU = mybir.AluOpType
AX = mybir.AxisListType

NEG_LOC = -2.0e38
NEG_FUT = -1.0e38
NEG_REP = -3.0e38
NEG_INF = -1.0e30


def make_consts(T, LW=64, DR=32, rope_base=10000.0):
    half = DR // 2
    th = (1.0 / rope_base ** (2.0 * np.arange(half) / DR)).astype(np.float64)
    c = {}
    c["ident"] = np.eye(128, dtype=np.float32)
    k = np.arange(128)[:, None]
    q = np.arange(128)[None, :]
    c["l0diag"] = np.where(q > k, NEG_INF, 0.0).astype(np.float32)
    r = np.arange(128)[:, None]
    cc = np.arange(128)[None, :]
    c["pool_loc"] = np.where((cc > r) & (cc <= r + LW), NEG_LOC,
                             0.0).astype(np.float32)
    fut = cc > r + LW
    c["pool_fmul"] = np.where(fut, 0.0, 1.0).astype(np.float32)
    c["pool_fadd"] = np.where(fut, NEG_FUT, c["pool_loc"]).astype(np.float32)
    col = np.arange(192)[None, :]
    c["mask_band"] = np.where((col >= r + 1) & (col <= r + LW), 0.0,
                              NEG_INF).astype(np.float32)
    okB = (r >= LW - 1) & (col >= r - LW + 1) & (col <= r)
    okA = (r < 95) & (col <= 95) & ((col > r) | (r < LW - 1))
    c["mask_band0"] = np.where(okB, 0.0, NEG_INF).astype(np.float32)
    c["mask_abs0"] = np.where(okA, 0.0, NEG_INF).astype(np.float32)
    t = np.arange(T)[:, None].astype(np.float64)
    c["qrotA"] = np.concatenate([np.cos(t * th), np.sin(t * th)],
                                axis=1).astype(np.float32)
    c["qrotB"] = np.concatenate([np.cos(2 * t * th), np.sin(2 * t * th)],
                                axis=1).astype(np.float32)
    s = np.arange(T)[None, :].astype(np.float64)
    c["krotB"] = np.concatenate([np.cos((s + LW - 1) * th[:, None]),
                                 np.sin((s + LW - 1) * th[:, None])],
                                axis=0).astype(np.float32)
    c["krotA"] = np.concatenate([np.cos(s * th[:, None]),
                                 np.sin(s * th[:, None])],
                                axis=0).astype(np.float32)
    rot4 = np.zeros((8, 32, 128), np.float64)
    for r in range(32):
        q, rl = r // 4, r % 4
        for d in range(16):
            cc_ = np.cos((LW + r) * th[d])
            ss_ = np.sin((LW + r) * th[d])
            rot4[q, d, 32 * rl + d] = cc_
            rot4[q, 16 + d, 32 * rl + d] = -ss_
            rot4[q, d, 32 * rl + 16 + d] = ss_
            rot4[q, 16 + d, 32 * rl + 16 + d] = cc_
    c["rot4"] = rot4.reshape(8 * 32, 128).astype(np.float32)
    rt = ((np.arange(32) % 16) * T).astype(np.float32)
    c["rtpr"] = np.repeat(rt.reshape(1, 32), 128, 0).astype(np.float32)
    return c


class MK:
    def __init__(self, cfg):
        self.cfg = cfg
        for k, v in cfg.items():
            if k != "consts":
                setattr(self, k, v)
        self.NT = self.T // 128
        self.TCW = min(512, self.T)
        self.DC = self.D // 128
        self.DH0 = self.D // self.H0
        self.E = 8
        self.DR = 32
        self.LW = 64
        self.nc = bass.Bass("TRN2", target_bir_lowering=False)
        self.es = ExitStack()
        self._stack = [self.es]
        self._sem = {}
        self._scnt = {}
        self._lib = None
        self._uid = 0
        self._gregs = {}
        for n in ("hw", "sw", "pe", "pe2", "act", "dve", "pool"):
            self._sem[n] = self.es.enter_context(self.nc.semaphore("s_" + n))
            self._scnt[n] = 0

    def greg(self, e, val):
        if val not in self._gregs:
            self._gregs[val] = e.to_reg(val)
        return self._gregs[val]

    def uid(self):
        self._uid += 1
        return self._uid

    def scope(self):
        class _S:
            def __enter__(s):
                s.es = ExitStack()
                self._stack.append(s.es)
                return s.es

            def __exit__(s, *a):
                self._stack.pop()
                return s.es.__exit__(*a)
        return _S()

    def sb(self, name, shape, dt=F32):
        return self._stack[-1].enter_context(
            self.nc.sbuf_tensor(f"{name}_{self.uid()}", shape, dt))

    def psum(self, name, shape, dt=F32):
        return self._stack[-1].enter_context(
            self.nc.psum_tensor(f"{name}_{self.uid()}", shape, dt))

    def din(self, name, shape, dt=F32):
        return self.nc.dram_tensor(name, shape, dt, kind="ExternalInput").ap()

    def dint(self, name, shape, dt=F32):
        return self.nc.dram_tensor(f"{name}_{self.uid()}", shape, dt,
                                   kind="Internal").ap()

    def dma(self, qn, eng, out, in_, **kw):
        self._scnt[qn] += 16
        return eng.dma_start(out, in_, **kw).then_inc(self._sem[qn], 16)

    def dwait(self, eng, qns=("hw", "sw")):
        for qn in qns:
            if self._scnt[qn]:
                eng.wait_ge(self._sem[qn], self._scnt[qn])

    def resv(self, en, n=1):
        out = [(self._sem[en], self._scnt[en] + i + 1) for i in range(n)]
        self._scnt[en] += n
        return out

    def fire(self, instr, ev):
        instr.then_inc(ev[0], 1)
        return instr

    def wait(self, eng, ev):
        eng.wait_ge(ev[0], ev[1])

    def seq(self, en, eng, fns):
        for i, fn in enumerate(fns):
            if i:
                eng.wait_ge(self._sem[en], self._scnt[en])
            ins = fn(eng)
            self._scnt[en] += 1
            ins.then_inc(self._sem[en], 1)

    def pipe2(self, items, prod="tensor", cons="vector", pfam="pe",
              cfam="dve"):
        n = len(items)
        evP = self.resv(pfam, n)
        evC = self.resv(cfam, n)
        with self.nc.Block() as block:
            def prod_body(t):
                for i, (pf, _) in enumerate(items):
                    if i >= 2:
                        self.wait(t, evC[i - 2])
                    self.fire(pf(t, i % 2), evP[i])

            def cons_body(v):
                for i, (_, cf) in enumerate(items):
                    self.wait(v, evP[i])
                    self.fire(cf(v, i % 2), evC[i])
            getattr(block, prod)(prod_body)
            getattr(block, cons)(cons_body)

    def lib(self, eng, which):
        if self._lib is not which:
            eng.load_library(which)
            self._lib = which

    def blk(self, **engs):
        with self.nc.Block() as block:
            for name, fn in engs.items():
                getattr(block, name)(fn)

    def declare_io(self):
        T, D, E = self.T, self.D, self.E
        DCKV, DCQ, H1, DH1, DR = self.DCKV, self.DCQ, self.H1, self.DH1, self.DR
        HID, RH = self.HID, self.RH
        d = self.din
        self.i_xT = d("xT", [D, T])
        self.i_wqkvT = d("wqkvT", [D, 3 * D])
        self.i_wprojT = d("wprojT", [D, D])
        self.i_Wr = d("Wr", [D, E])
        self.i_br = d("br", [1, E])
        self.i_W1 = d("W1", [E, D, D])
        self.i_b1 = d("b1", [E, D])
        self.i_W2 = d("W2", [E, D, D])
        self.i_b2 = d("b2", [E, D])
        self.i_c0 = d("c0", [1, D])
        self.i_wmod0 = d("wmod0", [D, 6 * D])
        self.i_Wdkv = d("Wdkv", [D, DCKV])
        self.i_Wuk = d("Wuk", [DCKV, H1 * DH1])
        self.i_WukT = d("WukT", [H1 * DH1, DCKV])
        self.i_Wuv = d("Wuv", [DCKV, H1 * DH1])
        self.i_Wdq = d("Wdq", [D, DCQ])
        self.i_Wuq = d("Wuq", [DCQ, H1 * DH1])
        self.i_Wqr = d("Wqr", [DCQ, H1 * DR])
        self.i_Wkr = d("Wkr", [D, DR])
        self.i_Wout = d("Wout", [H1 * DH1, D])
        self.i_idxwq = d("idxwq", [D, 4 * 64])
        self.i_idxwk = d("idxwk", [D, 64])
        self.i_idxw = d("idxw", [1, 4])
        self.i_w12T = d("w12T", [E, D, 2 * RH])
        self.i_w3T = d("w3T", [E, RH, D])
        self.i_sw1T = d("sw1T", [D, HID])
        self.i_sw2T = d("sw2T", [D, HID])
        self.i_sw3T = d("sw3T", [HID, D])
        self.i_routerT = d("routerT", [D, E])
        self.i_expbias = d("expbias", [1, E])
        self.i_c1 = d("c1", [1, D])
        self.i_wmod1 = d("wmod1", [D, 6 * D])
        self.CN = {k: d("cn_" + k, list(v.shape))
                   for k, v in self.cfg["consts"].items()}
        self.o_outT = self.nc.dram_tensor("outT", [D, T], F32,
                                          kind="ExternalOutput").ap()
        self.d_hh = self.dint("d_hh", [D, T], BF16)
        self.d_row = self.dint("d_row", [1, 4 * max(self.T, 128 * self.H0)])
        self.d_rec = self.dint("d_rec", [T, 256], BF16)
        self.d_mu = self.dint("d_mu", [T, 32], BF16)
        self.d_idx = self.dint("d_idx", [16, self.NT * 256], I16)
        self.d_g = self.dint("d_g", [HID, T], BF16)
        self.d_xs = self.dint("d_xs", [D, T])
        self.d_ge = self.dint("d_ge", [E, RH, T], BF16)

        self.X = self.sb("X", [128, self.DC, T])
        SW = max(T, 128 * self.H0)
        self.SA = self.sb("SA", [128, SW])
        self.SB_ = self.sb("SB", [128, SW])
        self.IDB = self.sb("IDB", [128, 128], BF16)
        self.IDF = self.sb("IDF", [128, 128], F32)
        self.ONESD = self.sb("ONESD", [128, 1], F32)
        self.ONE1 = self.sb("ONE1", [128, 1], F32)
        self.MOD = {n: self.sb("MOD" + n, [128, self.DC], F32)
                    for n in ("sa", "ca", "ga", "sh", "cb", "gb")}
        self.L0DIAG = self.sb("L0DIAG", [128, 128], F32)
        self.SAB16 = self.sb("SAB16", [128, 512], BF16)
        self.PS0 = self.psum("PS0", [128, 512], F32)
        self.PS1 = self.psum("PS1", [128, 512], F32)
        self.PSA = self.psum("PSA", [128, 128], F32)
        self.PSB = self.psum("PSB", [128, 128], F32)
        self.PSC = self.psum("PSC", [128, 128], F32)
        self.PSD = self.psum("PSD", [128, 128], F32)
        self.PSTB = self.psum("PSTB", [128, 128], BF16)
        self.PSTF = self.psum("PSTF", [128, 128], F32)

    def stage_load(self):
        def g(e):
            for c in range(self.DC):
                self.dma("sw", e, self.X[:, c, :],
                         self.i_xT[128 * c:128 * (c + 1), :])
            self.dma("sw", e, self.IDF[:], self.CN["ident"][:, :])
            self.dma("sw", e, self.L0DIAG[:], self.CN["l0diag"][:, :])
        self.blk(gpsimd=g)

        def v(e):
            self.dwait(e)
            e.memset(self.ONESD[:], 1.0 / self.D)
            e.memset(self.ONE1[:], 1.0)
            e.tensor_copy(self.IDB[:], self.IDF[:])
        self.blk(vector=v)

    def wload(self, e, buf, src_ap, col0, ncol, nk=None):
        nk = nk if nk is not None else src_ap.shape[0] // 128
        for kc in range(nk):
            self.dma("sw", e, buf[:, kc, :ncol],
                     src_ap[128 * kc:128 * (kc + 1), col0:col0 + ncol])

    def stage_mod(self, c_ap, wmod_ap):
        DC, D = self.DC, self.D
        with self.scope():
            cs = self.sb("cs", [128, DC], F32)
            csb = self.sb("csb", [128, DC], BF16)
            wbuf = self.sb("wmodb", [128, DC, 128], BF16)
            self.blk(gpsimd=lambda e: self.dma(
                "sw", e, cs[:, :], c_ap.rearrange("o (c p) -> p c", p=128),
                allow_slow_non_contiguous=True))
            self.blk(scalar=lambda e: (self.dwait(e),
                                       e.activation(cs[:], cs[:], AF.Silu)))
            self.blk(vector=lambda e: e.tensor_copy(csb[:], cs[:]))
            names = ["sa", "ca", "ga", "sh", "cb", "gb"]
            for gi in range(6):
                for mc in range(DC):
                    self.blk(gpsimd=lambda e, gi=gi, mc=mc: self.wload(
                        e, wbuf, wmod_ap, gi * D + 128 * mc, 128, DC))

                    def t(e, gi=gi, mc=mc):
                        self.dwait(e)
                        for kc in range(DC):
                            e.matmul(self.PSA[:, 0:1], wbuf[:, kc, :],
                                     csb[:, kc:kc + 1],
                                     start=(kc == 0), stop=(kc == DC - 1))
                    self.blk(tensor=t)
                    self.blk(vector=lambda e, gi=gi, mc=mc: e.tensor_copy(
                        self.MOD[names[gi]][:, mc:mc + 1], self.PSA[:, 0:1]))
            sgn = self.sb("sgn", [128, DC], F32)
            for nm in ("ga", "gb"):
                self.blk(vector=lambda e, nm=nm: e.tensor_scalar(
                    sgn[:], self.MOD[nm][:], 0.0, None, ALU.is_ge))
                self.blk(vector=lambda e, nm=nm: e.tensor_scalar(
                    sgn[:], sgn[:], 2.0, -1.0, ALU.mult, ALU.add))
                self.blk(scalar=lambda e, nm=nm: e.activation(
                    self.MOD[nm][:], self.MOD[nm][:], AF.Abs))
                self.blk(vector=lambda e, nm=nm: e.tensor_scalar_max(
                    self.MOD[nm][:], self.MOD[nm][:], 0.25))
                self.blk(vector=lambda e, nm=nm: e.tensor_mul(
                    self.MOD[nm][:], self.MOD[nm][:], sgn[:]))

    def stage_ln(self, cname, sname, keep_rows=False):
        T, DC = self.T, self.DC
        if keep_rows:
            self.ROWRK = self.sb("ROWRK", [128, T], BF16)
            self.ROWWK = self.sb("ROWWK", [128, T], BF16)
        with self.scope():
            MR = self.sb("MR", [1, 2 * T], F32)
            ROWR = self.ROWRK if keep_rows else self.sb("ROWR", [128, T], BF16)
            ROWW = self.ROWWK if keep_rows else self.sb("ROWW", [128, T], BF16)
            HT = self.sb("HT", [128, 2, T], BF16)
            W = self.TCW
            for tc in range(T // W):
                sl = slice(W * tc, W * (tc + 1))

                def t1(e, sl=sl):
                    for kc in range(DC):
                        e.matmul(self.PS0[0:1, :W], self.ONESD[:],
                                 self.X[:, kc, sl], start=(kc == 0),
                                 stop=(kc == DC - 1))
                self.blk(tensor=t1)
                self.blk(vector=lambda e, sl=sl: e.tensor_copy(
                    MR[0:1, sl], self.PS0[0:1, :W]))
                def sq_pf(s, par, kc, sl=sl):
                    buf = self.SA if par == 0 else self.SB_
                    return s.activation(buf[:, :W], self.X[:, kc, sl],
                                        AF.Square)

                def sq_cf(t, par, kc):
                    buf = self.SA if par == 0 else self.SB_
                    return t.matmul(self.PS1[0:1, :W], self.ONESD[:],
                                    buf[:, :W], start=(kc == 0),
                                    stop=(kc == DC - 1))
                self.pipe2([(lambda s, par, kc=kc: sq_pf(s, par, kc),
                             lambda t, par, kc=kc: sq_cf(t, par, kc))
                            for kc in range(DC)],
                           prod="scalar", cons="tensor", pfam="act",
                           cfam="pe")
                self.blk(vector=lambda e, tc=tc: e.tensor_copy(
                    MR[0:1, T + W * tc:T + W * (tc + 1)], self.PS1[0:1, :W]))
            self.blk(vector=lambda e: e.tensor_mul(
                self.SA[0:1, :T], MR[0:1, :T], MR[0:1, :T]))
            self.blk(vector=lambda e: e.tensor_sub(
                MR[0:1, T:], MR[0:1, T:], self.SA[0:1, :T]))
            self.blk(vector=lambda e: e.tensor_scalar_add(
                MR[0:1, T:], MR[0:1, T:], 1e-5))
            self.blk(scalar=lambda e: e.activation(
                MR[0:1, T:], MR[0:1, T:], AF.Sqrt))
            self.blk(vector=lambda e: e.reciprocal(MR[0:1, T:], MR[0:1, T:]))
            self.blk(vector=lambda e: e.tensor_mul(
                MR[0:1, :T], MR[0:1, :T], MR[0:1, T:]))

            def g1(e):
                self.dma("sw", e, self.d_row[:, :T], MR[0:1, T:])
                self.dma("sw", e, self.d_row[:, T:2 * T], MR[0:1, :T])
            self.blk(gpsimd=g1)

            def g2(e):
                self.dwait(e)
                self.dma("sw", e, ROWR[:, :],
                         bass.AP(self.d_row.tensor, 0, [[0, 128], [1, T]]))
                self.dma("sw", e, ROWW[:, :],
                         bass.AP(self.d_row.tensor, T, [[0, 128], [1, T]]))
            self.blk(gpsimd=g2)
            onepc = self.sb("onepc", [128, DC], F32)
            self.blk(vector=lambda e: (self.dwait(e), e.tensor_scalar_add(
                onepc[:], self.MOD[cname][:], 1.0)))
            evH = self.resv("dve", 3 * DC)
            hwbase = self._scnt["hw"]

            with self.nc.Block() as block:
                @block.vector
                def _(v):
                    for kc in range(DC):
                        p = kc % 2
                        if kc >= 2:
                            v.wait_ge(self._sem["hw"], hwbase + 16 * kc)
                        if kc >= 1:
                            self.wait(v, evH[3 * kc - 1])
                        self.fire(v.tensor_mul(
                            self.SA[:, :T], self.X[:, kc, :], ROWR[:, :]),
                            evH[3 * kc])
                        self.wait(v, evH[3 * kc])
                        self.fire(v.tensor_sub(
                            self.SA[:, :T], self.SA[:, :T], ROWW[:, :]),
                            evH[3 * kc + 1])
                        self.wait(v, evH[3 * kc + 1])
                        self.fire(v.tensor_scalar(
                            HT[:, p, :T], self.SA[:, :T],
                            onepc[:, kc:kc + 1],
                            self.MOD[sname][:, kc:kc + 1],
                            ALU.mult, ALU.add), evH[3 * kc + 2])

                @block.sync
                def _(e):
                    for kc in range(DC):
                        p = kc % 2
                        e.wait_ge(self._sem["dve"], evH[3 * kc + 2][1])
                        self.dma("hw", e,
                                 self.d_hh[128 * kc:128 * (kc + 1), :],
                                 HT[:, p, :T])
            self.blk(vector=lambda e: self.dwait(e))

    def stage_l0_attn(self):
        T, D, DC, H0, DH0, NT = self.T, self.D, self.DC, self.H0, self.DH0, self.NT
        dbg = getattr(self, "dbg_l0", 0)
        nc = self.nc
        scale0 = 1.0 / math.sqrt(DH0)
        hpc = 128 // DH0
        with self.scope():
            Kf = self.sb("Kf", [128, DC, T], BF16)
            Vt = self.sb("Vt", [128, NT, H0 * (DH0 + 1)], BF16)
            HHc = self.sb("HHc", [128, DC, self.TCW], BF16)
            WB = self.sb("WB", [128, DC, 128], BF16)
            Qf = self.sb("Qf", [128, DC, 128], BF16)
            attnT = self.sb("attnT", [128, DC, 128], BF16)
            Pb = self.sb("Pb", [128, T], BF16)
            rmax = self.sb("rmax0", [128, 4], F32)
            W = self.TCW
            for tc in range(T // W):
                self.blk(gpsimd=lambda e, tc=tc: [self.dma(
                    "sw", e, HHc[:, kc, :W],
                    self.d_hh[128 * kc:128 * (kc + 1), W * tc:W * (tc + 1)])
                    for kc in range(DC)])
                for oc in range(DC):
                    self.blk(gpsimd=lambda e, oc=oc: self.wload(
                        e, WB, self.i_wqkvT, D + 128 * oc, 128, DC))

                    def t(e, oc=oc, tc=tc):
                        self.dwait(e)
                        for kc in range(DC):
                            e.matmul(self.PS0[:, :W], WB[:, kc, :],
                                     HHc[:, kc, :W],
                                     start=(kc == 0), stop=(kc == DC - 1))
                    self.blk(tensor=t)
                    self.blk(vector=lambda e, oc=oc, tc=tc: e.tensor_copy(
                        Kf[:, oc, W * tc:W * (tc + 1)], self.PS0[:, :W]))
                for oc in range(DC):
                    self.blk(gpsimd=lambda e, oc=oc: self.wload(
                        e, WB, self.i_wqkvT, 2 * D + 128 * oc, 128, DC))
                    for q4 in range(W // 128):
                        kt = (W // 128) * tc + q4

                        def t2(e, oc=oc, q4=q4):
                            self.dwait(e)
                            for kc in range(DC):
                                e.matmul(self.PSA[:, :],
                                         HHc[:, kc, 128 * q4:128 * (q4 + 1)],
                                         WB[:, kc, :],
                                         start=(kc == 0), stop=(kc == DC - 1))
                        self.blk(tensor=t2)

                        def v2(e, kt=kt, oc=oc):
                            for hh in range(hpc):
                                h = oc * hpc + hh
                                e.tensor_copy(
                                    Vt[:, kt, h * (DH0 + 1):h * (DH0 + 1) + DH0],
                                    self.PSA[:, hh * DH0:(hh + 1) * DH0])
                        self.blk(vector=v2)
            if dbg == 1:
                return
            WQ = self.sb("WQ", [128, DC, D], BF16)
            WP = self.sb("WP", [128, DC, D], BF16)

            def gw(e):
                for kc in range(DC):
                    self.dma("sw", e, WQ[:, kc, :],
                             self.i_wqkvT[128 * kc:128 * (kc + 1), 0:D])
                    self.dma("sw", e, WP[:, kc, :],
                             self.i_wprojT[128 * kc:128 * (kc + 1), :])
            self.blk(gpsimd=gw)

            qtN = NT if dbg not in (3, 4) else (2 if dbg == 3 else 4)
            for qt in range(qtN):
                qsl = slice(128 * qt, 128 * (qt + 1))
                self.blk(gpsimd=lambda e: [self.dma(
                    "sw", e, HHc[:, kc, :128],
                    self.d_hh[128 * kc:128 * (kc + 1), qsl])
                    for kc in range(DC)])
                def qb_pf(t, par, oc):
                    if oc == 0:
                        self.dwait(t)
                    ps = self.PSA if par == 0 else self.PSB
                    for kc in range(DC):
                        mm = t.matmul(ps[:, :],
                                      WQ[:, kc, 128 * oc:128 * (oc + 1)],
                                      HHc[:, kc, :128],
                                      start=(kc == 0), stop=(kc == DC - 1))
                    return mm

                def qb_cf(v, par, oc):
                    ps = self.PSA if par == 0 else self.PSB
                    return v.tensor_copy(Qf[:, oc, :], ps[:, :])
                self.pipe2([(lambda t, par, oc=oc: qb_pf(t, par, oc),
                             lambda v, par, oc=oc: qb_cf(v, par, oc))
                            for oc in range(DC)])
                if dbg == 2:
                    continue
                nkt = qt + 1
                Wk = 128 * nkt
                resv_h = []
                for h in range(H0):
                    r = {}
                    r["S"] = self.resv("pe", nkt)
                    r["T"] = self.resv("pe", nkt)
                    r["OT"] = self.resv("pe", 1)[0]
                    r["PV"] = self.resv("pe2", nkt)
                    r["C"] = self.resv("dve", nkt)
                    r["R1"] = self.resv("dve", 1)[0]
                    r["R2"] = self.resv("dve", 1)[0]
                    r["RC"] = self.resv("dve", 1)[0]
                    r["PT"] = self.resv("dve", nkt)
                    r["N"] = self.resv("dve", 1)[0]
                    r["A"] = self.resv("dve", 1)[0]
                    r["X"] = self.resv("act", 1)[0]
                    resv_h.append(r)
                PTb = [self.SAB16[:, 128:256], self.SAB16[:, 256:384]]

                with self.nc.Block() as block:
                    @block.tensor
                    def _(t, qt=qt, nkt=nkt, resv_h=resv_h):
                        lastC = {0: None, 1: None}
                        prevN = prevA = None
                        for h in range(H0):
                            r = resv_h[h]
                            oc, hh = h // hpc, h % hpc
                            hr = slice(hh * DH0, (hh + 1) * DH0)
                            for kt in range(nkt):
                                ps = self.PSA if kt % 2 == 0 else self.PSB
                                if lastC[kt % 2] is not None:
                                    self.wait(t, lastC[kt % 2])
                                self.fire(t.matmul(
                                    ps[:, :], Qf[hr, oc, :],
                                    Kf[hr, oc, 128 * kt:128 * (kt + 1)],
                                    start=True, stop=True), r["S"][kt])
                                lastC[kt % 2] = r["C"][kt]
                            self.wait(t, r["X"])
                            if prevA is not None:
                                self.wait(t, prevA)
                            self.fire(t.transpose(
                                self.PSTB[:, :], Pb[:, 0:128],
                                self.IDB[:, :]), r["T"][0])
                            for kt in range(nkt):
                                self.wait(t, r["PT"][kt])
                                if kt == 0 and prevN is not None:
                                    self.wait(t, prevN)
                                self.fire(t.matmul(
                                    self.PSC[:, :DH0], PTb[kt % 2],
                                    Vt[:, kt,
                                       h * (DH0 + 1):h * (DH0 + 1) + DH0],
                                    start=(kt == 0), stop=(kt == nkt - 1)),
                                    r["PV"][kt])
                                if kt + 1 < nkt:
                                    self.fire(t.transpose(
                                        self.PSTB[:, :],
                                        Pb[:, 128 * (kt + 1):128 * (kt + 2)],
                                        self.IDB[:, :]), r["T"][kt + 1])
                            self.wait(t, r["N"])
                            self.fire(t.transpose(
                                self.PSTB[:DH0, :], self.SAB16[:, 0:DH0],
                                self.IDB[:, :]), r["OT"])
                            prevN, prevA = r["N"], r["A"]

                    @block.vector
                    def _(v, qt=qt, nkt=nkt, Wk=Wk, resv_h=resv_h):
                        prevX = None
                        for h in range(H0):
                            r = resv_h[h]
                            oc, hh = h // hpc, h % hpc
                            hr = slice(hh * DH0, (hh + 1) * DH0)
                            for kt in range(nkt):
                                ps = self.PSA if kt % 2 == 0 else self.PSB
                                self.wait(v, r["S"][kt])
                                if kt == 0 and prevX is not None:
                                    self.wait(v, prevX)
                                if kt == nkt - 1:
                                    self.fire(v.tensor_tensor(
                                        self.SA[:, 128 * kt:128 * (kt + 1)],
                                        ps[:, :], self.L0DIAG[:, :],
                                        ALU.add), r["C"][kt])
                                else:
                                    self.fire(v.tensor_copy(
                                        self.SA[:, 128 * kt:128 * (kt + 1)],
                                        ps[:, :]), r["C"][kt])
                            self.wait(v, r["C"][nkt - 1])
                            self.fire(v.tensor_reduce(
                                rmax[:, 0:1], self.SA[:, :Wk], AX.X,
                                ALU.max), r["R1"])
                            self.wait(v, r["R1"])
                            self.fire(v.tensor_scalar(
                                rmax[:, 1:2], rmax[:, 0:1], -scale0, None,
                                ALU.mult), r["R2"])
                            self.wait(v, r["X"])
                            self.fire(v.reciprocal(
                                rmax[:, 3:4], rmax[:, 2:3]), r["RC"])
                            for kt in range(nkt):
                                self.wait(v, r["T"][kt])
                                if kt >= 2:
                                    self.wait(v, r["PV"][kt - 2])
                                self.fire(v.tensor_copy(
                                    PTb[kt % 2], self.PSTB[:, :]),
                                    r["PT"][kt])
                            self.wait(v, r["PV"][nkt - 1])
                            self.wait(v, r["RC"])
                            self.fire(v.tensor_scalar(
                                self.SAB16[:, 0:DH0], self.PSC[:, :DH0],
                                rmax[:, 3:4], None, ALU.mult), r["N"])
                            self.wait(v, r["OT"])
                            self.fire(v.tensor_copy(
                                attnT[hr, oc, :], self.PSTB[:DH0, :]),
                                r["A"])
                            prevX = r["X"]

                    @block.scalar
                    def _(s, qt=qt, nkt=nkt, Wk=Wk, resv_h=resv_h):
                        prevA = None
                        for h in range(H0):
                            r = resv_h[h]
                            self.wait(s, r["R2"])
                            if prevA is not None:
                                self.wait(s, prevA)
                            self.fire(s.activation(
                                Pb[:, :Wk], self.SA[:, :Wk], AF.Exp,
                                bias=rmax[:, 1:2], scale=scale0,
                                accum_out=rmax[:, 2:3]), r["X"])
                            prevA = r["A"]
                for oc in range(DC):
                    self.blk(tensor=lambda e, oc=oc: [e.matmul(
                        self.PSA[:, :], WP[:, kc, 128 * oc:128 * (oc + 1)],
                        attnT[:, kc, :], start=(kc == 0), stop=(kc == DC - 1))
                        for kc in range(DC)])
                    self.blk(vector=lambda e, oc=oc, qsl=qsl:
                             e.scalar_tensor_tensor(
                                 self.X[:, oc, qsl], self.PSA[:, :],
                                 self.MOD["ga"][:, oc:oc + 1],
                                 self.X[:, oc, qsl], ALU.mult, ALU.add))

    def stage_l0_ffn(self):
        T, D, DC, NT, E = self.T, self.D, self.DC, self.NT, self.E
        dbg = getattr(self, "dbg_ffn", 0)
        with self.scope():
            HHc = self.sb("HHc", [128, DC, 128], BF16)
            h0col = self.sb("h0col", [128, DC], F32)
            h0bf = self.sb("h0bf", [128, DC], BF16)
            sc8 = self.sb("sc8", [128, 8], F32)
            Wrb = self.sb("Wrb", [128, DC, 8], F32)
            eidx = self.sb("eidx", [128, 2], F32)
            wbase = self.sb("wbase", [128, 1], F32)
            widx = self.sb("widx", [128, 1], I32)
            iotf = self.sb("iotf", [128, 1], F32)
            W1G = self.sb("W1G", [128, DC, D], BF16)
            W2G = self.sb("W2G", [128, DC, D], BF16)
            WSTG = self.sb("WSTG", [128, D], F32)
            BCOL = self.sb("BCOL", [128, DC, 2], F32)
            HTI = self.sb("HTI", [128, DC, 128], BF16)
            u8 = self.sb("u8", [128, 8], U32)
            self.blk(gpsimd=lambda e: self.dma(
                "sw", e, h0bf[:, :],
                self.d_hh.rearrange("(c p) t -> p c t", p=128)[:, :, 0:1]
                .rearrange("p c o -> p (c o)"),
                allow_slow_non_contiguous=True))
            self.blk(vector=lambda e: (self.dwait(e),
                                       e.tensor_copy(h0col[:], h0bf[:])))
            self.blk(gpsimd=lambda e: [self.dma(
                "sw", e, Wrb[:, kc, :],
                self.i_Wr[128 * kc:128 * (kc + 1), :]) for kc in range(DC)])

            def t(e):
                self.dwait(e)
                for kc in range(DC):
                    e.matmul(self.PSA[0:1, :8], h0col[:, kc:kc + 1],
                             Wrb[:, kc, :], start=(kc == 0),
                             stop=(kc == DC - 1))
            self.blk(tensor=t)
            self.blk(gpsimd=lambda e: self.dma(
                "sw", e, sc8[0:1, :], self.i_br[:, :]))
            self.blk(vector=lambda e: (self.dwait(e), e.tensor_tensor(
                sc8[0:1, :], sc8[0:1, :], self.PSA[0:1, :8], ALU.add)))
            self.blk(vector=lambda e: e.max(self.SA[0:1, :8], sc8[0:1, :8]))
            self.blk(vector=lambda e: e.max_index(
                u8[0:1, :], self.SA[0:1, :8], sc8[0:1, :8]))
            self.blk(vector=lambda e: e.tensor_copy(
                eidx[0:1, :2], u8[0:1, :2]))
            self.blk(gpsimd=lambda e: self.dma(
                "sw", e, self.d_row[:, 0:2], eidx[0:1, :2]))
            self.blk(gpsimd=lambda e: (self.dwait(e), self.dma(
                "sw", e, eidx[:, :],
                bass.AP(self.d_row.tensor, 0, [[0, 128], [1, 2]]))))
            BASE16 = self.sb("BASE16", [128, 8], F32)
            IDX16 = self.sb("IDX16", [128, 8], I16)
            ECOL = self.sb("ECOL", [128, 1], F32)
            self.blk(gpsimd=lambda e: (
                self.lib(e, LC.standard),
                e.iota(self.SA[0:1, 0:128], [[1, 128]], channel_multiplier=0,
                       allow_small_or_imprecise_dtypes=True)))
            self.blk(gpsimd=lambda e: (self.dwait(e), self.dma(
                "sw", e, self.d_row[:, 0:128], self.SA[0:1, 0:128])))
            self.blk(gpsimd=lambda e: (self.dwait(e), [self.dma(
                "sw", e, BASE16[16 * g:16 * (g + 1), :],
                bass.AP(self.d_row.tensor, 0, [[1, 16], [16, 8]]),
                allow_slow_non_contiguous=True) for g in range(8)]))
            self.blk(vector=lambda e: self.dwait(e))
            if dbg == 1:
                return
            for ke in range(2):
                self.blk(vector=lambda e, ke=ke: e.tensor_scalar(
                    ECOL[:, :], eidx[:, ke:ke + 1], float(D), None, ALU.mult))
                for wi, (w_ap, WG) in enumerate([(self.i_W1, W1G),
                                                 (self.i_W2, W2G)]):
                    w_flat = w_ap.rearrange("e a b -> (e a) b")
                    for kc in range(DC):
                        self.blk(vector=lambda e, kc=kc: self.seq("dve", e, [
                            lambda v: v.tensor_scalar(
                                self.SA[:, 0:8], BASE16[:, :],
                                float(128 * kc), ECOL[:, 0:1],
                                ALU.add, ALU.add),
                            lambda v: v.tensor_copy(IDX16[:, :],
                                                    self.SA[:, 0:8]),
                        ]))

                        def gi(e, w_flat=w_flat):
                            self.lib(e, LC.mlp)
                            self._scnt["sw"] += 16
                            e.dma_gather(
                                WSTG[:].rearrange("p (s d) -> p s d", d=D),
                                w_flat[:, :], IDX16[:, :], 128,
                                self.greg(e, 128), D, elem_step=D,
                            ).then_inc(self._sem["sw"], 16)
                        self.blk(gpsimd=gi)
                        self.blk(vector=lambda e, WG=WG, kc=kc: (
                            self.dwait(e),
                            e.tensor_copy(WG[:, kc, :], WSTG[:, :])))
                if dbg == 2:
                    continue
                self.blk(vector=lambda e, ke=ke: e.tensor_copy(
                    widx[:, :], eidx[:, ke:ke + 1]))
                self.blk(vector=lambda e, ke=ke: self.seq("dve", e, [
                    lambda v: v.tensor_copy(
                        self.SA[:, 0:8],
                        bass.AP(eidx, ke, [[2, 128], [0, 8]])),
                    lambda v: v.tensor_copy(IDX16[:, :], self.SA[:, 0:8]),
                ]))
                for wi, b_ap in enumerate([self.i_b1, self.i_b2]):
                    def gb(e, b_ap=b_ap):
                        self.lib(e, LC.mlp)
                        self._scnt["sw"] += 16
                        e.dma_gather(
                            WSTG[:].rearrange("p (s d) -> p s d", d=D),
                            b_ap[:, :], IDX16[:, :], 128,
                            self.greg(e, 128), D, elem_step=D,
                        ).then_inc(self._sem["sw"], 16)
                    self.blk(gpsimd=gb)
                    self.blk(vector=lambda e: self.dwait(e))
                    for kc in range(DC):
                        self.blk(tensor=lambda e, kc=kc: e.transpose(
                            self.PSD[:, 0:1],
                            WSTG[0:1, 128 * kc:128 * (kc + 1)],
                            self.IDF[0:1, 0:1]))
                        self.blk(vector=lambda e, kc=kc, wi=wi: e.tensor_copy(
                            BCOL[:, kc, wi:wi + 1], self.PSD[:, 0:1]))
                for tt in range(0 if dbg not in (3, 4, 5) else NT, NT):
                    tsl = slice(128 * tt, 128 * (tt + 1))
                    self.blk(gpsimd=lambda e, tsl=tsl: [self.dma(
                        "sw", e, HHc[:, kc, :],
                        self.d_hh[128 * kc:128 * (kc + 1), tsl])
                        for kc in range(DC)])
                    def f1_pf(t, par, oc):
                        if oc == 0:
                            self.dwait(t)
                        ps = self.PSA if par == 0 else self.PSB
                        for kc in range(DC):
                            mm = t.matmul(ps[:, :], W1G[:, kc,
                                          128 * oc:128 * (oc + 1)],
                                          HHc[:, kc, :], start=(kc == 0),
                                          stop=(kc == DC - 1))
                        return mm

                    def f1_cf(s, par, oc):
                        ps = self.PSA if par == 0 else self.PSB
                        return s.activation(HTI[:, oc, :], ps[:, :], AF.Gelu,
                                            bias=BCOL[:, oc, 0:1])
                    self.pipe2([(lambda t, par, oc=oc: f1_pf(t, par, oc),
                                 lambda s, par, oc=oc: f1_cf(s, par, oc))
                                for oc in range(DC)],
                               cons="scalar", cfam="act")
                    for oc in range(DC):
                        def t2(e, oc=oc):
                            for kc in range(DC):
                                e.matmul(self.PSA[:, :], W2G[:, kc,
                                         128 * oc:128 * (oc + 1)],
                                         HTI[:, kc, :], start=(kc == 0),
                                         stop=(kc == DC - 1))
                        self.blk(tensor=t2)
                        self.blk(vector=lambda e, oc=oc, tsl=tsl:
                                 self.seq("dve", e, [
                                     lambda v: v.tensor_scalar(
                                         self.SA[:, :128], self.PSA[:, :],
                                         BCOL[:, oc, 1:2], 0.5,
                                         ALU.add, ALU.mult),
                                     lambda v: v.scalar_tensor_tensor(
                                         self.X[:, oc, tsl],
                                         self.SA[:, :128],
                                         self.MOD["gb"][:, oc:oc + 1],
                                         self.X[:, oc, tsl],
                                         ALU.mult, ALU.add),
                                 ]))

    def stage_l1_attn(self):
        dbg = getattr(self, "dbg_l1", 0)
        T, D, DC, NT = self.T, self.D, self.DC, self.NT
        DCKV, DCQ, H1, DH1, DR, LW = (self.DCKV, self.DCQ, self.H1, self.DH1,
                                      self.DR, self.LW)
        KVC = DCKV // 128
        CQC = DCQ // 128
        KC = H1 * DH1 // 128
        hp1 = 128 // DH1
        nc = self.nc
        scale1 = 1.0 / math.sqrt(DH1 + DR)
        W = self.TCW
        with self.scope():
            d_kf = self.dint("d_kf", [H1 * DH1 // 128, T, 128], BF16)
            d_kfT = self.dint("d_kfT", [H1 * DH1, T], BF16)
            d_prA = self.dint("d_prA", [16 * T, 128], BF16)
            d_prB = self.dint("d_prB", [16 * T, 128], BF16)
            PADR = NT * 256 + 64
            d_i1 = self.dint("d_i1", [16, PADR], I16)
            d_i2 = self.dint("d_i2", [16, PADR], I16)
            Wdq = self.sb("Wdq", [128, DC, DCQ], BF16)
            Wuq = self.sb("Wuq", [128, CQC, H1 * DH1], BF16)
            Wqr = self.sb("Wqr", [128, CQC, H1 * DR], BF16)
            Wkr = self.sb("Wkr", [128, DC, DR], BF16)
            Iwq = self.sb("Iwq", [128, DC, 256], BF16)
            Iwk = self.sb("Iwk", [128, DC, 64], BF16)
            Wuv = self.sb("Wuv", [128, KVC, H1 * DH1], BF16)
            WoutB = self.sb("WoutB", [128, KC, D], BF16)
            ROT4 = self.sb("ROT4", [32, 8, 128], BF16)
            wcol = self.sb("wcol", [128, 4], F32)

            def gw(e):
                for kc in range(DC):
                    self.dma("sw", e, Wdq[:, kc, :], self.i_Wdq[128 * kc:128 * (kc + 1), :])
                    self.dma("sw", e, Wkr[:, kc, :], self.i_Wkr[128 * kc:128 * (kc + 1), :])
                    self.dma("sw", e, Iwq[:, kc, :], self.i_idxwq[128 * kc:128 * (kc + 1), :])
                    self.dma("sw", e, Iwk[:, kc, :], self.i_idxwk[128 * kc:128 * (kc + 1), :])
                for kc in range(KVC):
                    self.dma("sw", e, Wuv[:, kc, :], self.i_Wuv[128 * kc:128 * (kc + 1), :])
                for kc in range(CQC):
                    self.dma("sw", e, Wuq[:, kc, :], self.i_Wuq[128 * kc:128 * (kc + 1), :])
                    self.dma("sw", e, Wqr[:, kc, :], self.i_Wqr[128 * kc:128 * (kc + 1), :])
                for kc in range(KC):
                    self.dma("sw", e, WoutB[:, kc, :], self.i_Wout[128 * kc:128 * (kc + 1), :])
                for q_ in range(8):
                    self.dma("sw", e, ROT4[:, q_, :],
                             self.CN["rot4"][32 * q_:32 * (q_ + 1), :])
                self.dma("sw", e, self.d_row[:, 0:4], self.i_idxw[:, :])
            self.blk(gpsimd=gw)
            self.blk(gpsimd=lambda e: (self.dwait(e), self.dma(
                "sw", e, wcol[:, :],
                bass.AP(self.d_row.tensor, 0, [[0, 128], [1, 4]]))))
            MBAND = self.sb("MBAND", [128, 192], F32)
            MBAND0 = self.sb("MBAND0", [128, 192], F32)
            MABS0 = self.sb("MABS0", [128, 192], F32)
            PLOC = self.sb("PLOC", [128, 128], F32)
            PFUT = self.sb("PFUT", [128, 128], F32)
            RTPR = self.sb("RTPR", [128, 32], F32)

            def gc(e):
                self.dma("sw", e, MBAND[:], self.CN["mask_band"][:, :])
                self.dma("sw", e, MBAND0[:], self.CN["mask_band0"][:, :])
                self.dma("sw", e, MABS0[:], self.CN["mask_abs0"][:, :])
                self.dma("sw", e, PLOC[:], self.CN["pool_fmul"][:, :])
                self.dma("sw", e, PFUT[:], self.CN["pool_fadd"][:, :])
                self.dma("sw", e, RTPR[:], self.CN["rtpr"][:, :])
            self.blk(gpsimd=gc)
            self.blk(vector=lambda e: self.dwait(e))

            HHc = self.sb("HHc", [128, DC, W], BF16)
            ckvtm = self.sb("ckvtm", [128, NT, DCKV], BF16)
            kiT = self.sb("kiT", [64, T], BF16)
            mB = self.sb("mB", [32, T], BF16)
            mA = self.sb("mA", [32, 192], BF16)
            with self.scope():
                ckvT = self.sb("ckvT", [128, KVC, T], BF16)
                Wdkv = self.sb("Wdkv", [128, DC, DCKV], BF16)
                Wuk = self.sb("Wuk", [128, KVC, H1 * DH1], BF16)
                self.blk(gpsimd=lambda e: ([self.dma(
                    "sw", e, Wdkv[:, kc, :],
                    self.i_Wdkv[128 * kc:128 * (kc + 1), :]) for kc in range(DC)],
                    [self.dma("sw", e, Wuk[:, kc, :],
                              self.i_Wuk[128 * kc:128 * (kc + 1), :])
                     for kc in range(KVC)]))
                self.blk(vector=lambda e: self.dwait(e))
                muT = self.sb("muT", [32, T], BF16)
                for tc in range(T // W):
                    sl = slice(W * tc, W * (tc + 1))
                    self.blk(gpsimd=lambda e, sl=sl: [self.dma(
                        "sw", e, HHc[:, kc, :W],
                        self.d_hh[128 * kc:128 * (kc + 1), sl]) for kc in range(DC)])
                    for oc in range(KVC):
                        self.blk(tensor=lambda e, oc=oc: (self.dwait(e), [e.matmul(
                            self.PS0[:, :W], Wdkv[:, kc, 128 * oc:128 * (oc + 1)],
                            HHc[:, kc, :W], start=(kc == 0), stop=(kc == DC - 1))
                            for kc in range(DC)]))
                        self.blk(vector=lambda e, oc=oc, sl=sl: e.tensor_copy(
                            ckvT[:, oc, sl], self.PS0[:, :W]))
                    def tki(e):
                        for kc in range(DC):
                            e.matmul(self.PS1[:64, :W], Iwk[:, kc, :],
                                     HHc[:, kc, :W], start=(kc == 0),
                                     stop=(kc == DC - 1))
                    self.blk(tensor=tki)
                    self.blk(vector=lambda e, sl=sl: e.tensor_copy(
                        kiT[:, sl], self.PS1[:64, :W]))
                    def tkr(e):
                        for kc in range(DC):
                            e.matmul(self.PS1[:32, :W], Wkr[:, kc, :],
                                     HHc[:, kc, :W], start=(kc == 0),
                                     stop=(kc == DC - 1))
                    self.blk(tensor=tkr)
                    self.blk(scalar=lambda e: e.activation(
                        self.SA[:32, :W], self.PS1[:32, :W], AF.Abs))
                    self.blk(scalar=lambda e: e.activation(
                        self.SA[:32, :W], self.SA[:32, :W], AF.Exp, scale=-1.0))
                    self.blk(vector=lambda e: e.tensor_scalar_add(
                        self.SA[:32, :W], self.SA[:32, :W], 1.0))
                    self.blk(scalar=lambda e: e.activation(
                        self.SA[:32, :W], self.SA[:32, :W], AF.Ln))
                    self.blk(vector=lambda e, sl=sl: e.scalar_tensor_tensor(
                        muT[:, sl], self.PS1[:32, :W], 0.0,
                        self.SA[:32, :W], ALU.max, ALU.add))
                kfC = self.sb("kfC", [128, 2, W], BF16)
                for tc in range(T // W):
                    sl = slice(W * tc, W * (tc + 1))
                    for oc in range(KC):
                        self.blk(tensor=lambda e, oc=oc, sl=sl: [e.matmul(
                            self.PS0[:, :W], Wuk[:, kc, 128 * oc:128 * (oc + 1)],
                            ckvT[:, kc, sl], start=(kc == 0), stop=(kc == KVC - 1))
                            for kc in range(KVC)])
                        self.blk(vector=lambda e, oc=oc: (
                            self.dwait(e, ("hw",)),
                            e.tensor_copy(kfC[:, oc % 2, :], self.PS0[:, :W])))
                        self.blk(sync=lambda e, oc=oc, sl=sl: self.dma(
                            "hw", e, d_kfT[128 * oc:128 * (oc + 1), sl],
                            kfC[:, oc % 2, :]))
                mu2T = self.sb("mu2T", [16, T], BF16)
                KRc = self.sb("KRc", [16, T], BF16)
                KRs = self.sb("KRs", [16, T], BF16)
                MH2 = self.sb("MH2", [16, T], BF16)
                self.blk(gpsimd=lambda e: (
                    self.dma("sw", e, KRc[:, :], self.CN["krotB"][0:16, :]),
                    self.dma("sw", e, KRs[:, :], self.CN["krotB"][16:32, :]),
                    self.dma("sw", e, mu2T[:, :].bitcast(F32) if False else
                             mu2T[:, :], bass.AP(muT, 16 * muT.shape[1],
                                                 [[muT.shape[1], 16], [1, T]])))
                ) if False else None
                def loadkr(which):
                    self.blk(gpsimd=lambda e: (
                        self.dma("sw", e, KRc[:, :T if which == "krotB" else 192],
                                 self.CN[which][0:16, :T if which == "krotB" else 192]),
                        self.dma("sw", e, KRs[:, :T if which == "krotB" else 192],
                                 self.CN[which][16:32, :T if which == "krotB" else 192]),
                    ))
                self.blk(gpsimd=lambda e: self.dma(
                    "sw", e, mu2T[:, :],
                    bass.AP(muT, 16 * muT.shape[1],
                            [[muT.shape[1], 16], [1, T]])))
                def rot_apply(dst, n):
                    self.blk(vector=lambda e: (self.dwait(e), self.seq("dve", e, [
                        lambda v: v.tensor_mul(self.SA[:16, :n], muT[0:16, :n], KRc[:16, :n]),
                        lambda v: v.tensor_mul(self.SB_[:16, :n], mu2T[0:16, :n], KRs[:16, :n]),
                        lambda v: v.tensor_sub(self.SA[:16, :n], self.SA[:16, :n], self.SB_[:16, :n]),
                        lambda v: v.tensor_copy(dst[0:16, :n], self.SA[:16, :n]),
                        lambda v: v.tensor_mul(self.SA[:16, :n], muT[0:16, :n], KRs[:16, :n]),
                        lambda v: v.tensor_mul(self.SB_[:16, :n], mu2T[0:16, :n], KRc[:16, :n]),
                        lambda v: v.tensor_add(self.SA[:16, :n], self.SA[:16, :n], self.SB_[:16, :n]),
                        lambda v: v.tensor_copy(MH2[0:16, :n], self.SA[:16, :n]),
                    ])))
                    self.blk(gpsimd=lambda e: self.dma(
                        "sw", e,
                        bass.AP(dst, 16 * dst.shape[1], [[dst.shape[1], 16], [1, n]]),
                        MH2[0:16, :n]))
                    self.blk(vector=lambda e: self.dwait(e))
                loadkr("krotB")
                rot_apply(mB, T)
                loadkr("krotA")
                rot_apply(mA, 192)
                for kt in range(NT):
                    for oc in range(KC):
                        self.blk(gpsimd=lambda e, kt=kt, oc=oc: self.dma(
                            "sw", e, kfC[:, 0, :128],
                            d_kfT[128 * oc:128 * (oc + 1),
                                  128 * kt:128 * (kt + 1)]))
                        self.blk(tensor=lambda e: (self.dwait(e), e.transpose(
                            self.PSTB[:, :], kfC[:, 0, :128], self.IDB[:, :])))
                        self.blk(vector=lambda e: e.tensor_copy(
                            self.SAB16[:, :128], self.PSTB[:, :]))
                        self.blk(sync=lambda e, kt=kt, oc=oc: self.dma(
                            "hw", e, d_kf[oc, 128 * kt:128 * (kt + 1), :],
                            self.SAB16[:, :128]))
                        self.blk(vector=lambda e: self.dwait(e, ("hw",)))
                    for oc in range(KVC):
                        self.blk(tensor=lambda e, kt=kt, oc=oc: e.transpose(
                            self.PSTB[:, :], ckvT[:, oc, 128 * kt:128 * (kt + 1)],
                            self.IDB[:, :]))
                        self.blk(vector=lambda e, kt=kt, oc=oc: e.tensor_copy(
                            ckvtm[:, kt, 128 * oc:128 * (oc + 1)], self.PSTB[:, :]))
                muB = self.sb("muB", [32, T], BF16)
                self.blk(vector=lambda e: (e.memset(self.SAB16[:, :], 0.0),
                                           e.tensor_copy(muB[:, :], muT[:, :])))
                def zfill(e):
                    CH = 128 * 512
                    tot = 16 * T * 128
                    for tgt in (d_prA, d_prB):
                        fl = tgt.rearrange("a b -> (a b)")
                        for off in range(0, tot, CH):
                            self.dma("sw", e, fl[off:off + CH].rearrange(
                                "(p n) -> p n", p=128), self.SAB16[:, :])
                self.blk(gpsimd=zfill)
                self.blk(vector=lambda e: self.dwait(e))
                for q in range(8):
                    for kt in range(NT):
                        self.blk(tensor=lambda e, q=q, kt=kt: e.matmul(
                            self.PSA[:, :], muB[:, 128 * kt:128 * (kt + 1)],
                            ROT4[:, q, :], start=True, stop=True))
                        self.blk(vector=lambda e: e.tensor_copy(
                            self.SAB16[:, :128], self.PSA[:, :]))

                        def wpr(e, q=q, kt=kt):
                            tgt = d_prA if q < 4 else d_prB
                            r16 = (4 * q) % 16
                            self.dma("hw", e,
                                     bass.AP(tgt.tensor,
                                             (r16 * T + 128 * kt) * 128,
                                             [[128, 128], [T * 128, 4],
                                              [1, 32]]),
                                     bass.AP(self.SAB16, 0,
                                             [[512, 128], [32, 4], [1, 32]]))
                        self.blk(sync=wpr)
                        self.blk(vector=lambda e: self.dwait(e, ("hw",)))
            if dbg == 1:
                return
            SIGF = self.sb("SIGF", [128, NT * 32], F32)
            SIGI = self.sb("SIGI", [128, NT * 32], I16)
            SIGP = self.sb("SIGP", [128, NT * 32], I16)
            EMK = self.sb("EMK", [128, NT * 32], F32)
            mx8 = self.sb("mx8", [128, 8], F32)
            u8 = self.sb("u8", [128, 8], U32)
            qiT = self.sb("qiT", [64, 4, 128], BF16)
            for tt in range(NT):
                Wp = 128 * tt + 64
                tsl = slice(128 * tt, 128 * (tt + 1))
                self.blk(gpsimd=lambda e, tsl=tsl: [self.dma(
                    "sw", e, HHc[:, kc, :128],
                    self.d_hh[128 * kc:128 * (kc + 1), tsl])
                    for kc in range(DC)])
                for h in range(4):
                    self.blk(tensor=lambda e, h=h: (self.dwait(e), [e.matmul(
                        self.PSA[:64, :], Iwq[:, kc, 64 * h:64 * (h + 1)],
                        HHc[:, kc, :128], start=(kc == 0), stop=(kc == DC - 1))
                        for kc in range(DC)]))
                    self.blk(vector=lambda e, h=h: e.tensor_copy(
                        qiT[:, h, :], self.PSA[:64, :]))
                nch = (Wp + 511) // 512
                for ci in range(nch):
                    cw = min(512, Wp - 512 * ci)
                    csl = slice(512 * ci, 512 * ci + cw)
                    for h in range(4):
                        self.blk(tensor=lambda e, h=h, csl=csl, cw=cw: e.matmul(
                            self.PS0[:, :cw], qiT[:, h, :], kiT[:, csl],
                            start=True, stop=True))
                        if h == 0:
                            self.blk(vector=lambda e, csl=csl, cw=cw: e.tensor_scalar(
                                self.SA[:, csl], self.PS0[:, :cw], 0.0,
                                wcol[:, 0:1], ALU.max, ALU.mult))
                        else:
                            self.blk(scalar=lambda e, cw=cw: e.activation(
                                self.SB_[:, :cw], self.PS0[:, :cw], AF.Relu))
                            self.blk(vector=lambda e, h=h, csl=csl, cw=cw:
                                     e.scalar_tensor_tensor(
                                         self.SA[:, csl], self.SB_[:, :cw],
                                         wcol[:, h:h + 1], self.SA[:, csl],
                                         ALU.mult, ALU.add))
                wof = Wp - 128
                msl = slice(max(0, wof), Wp)
                mc0 = max(0, -wof)
                mw = msl.stop - msl.start
                self.blk(vector=lambda e, msl=msl, mc0=mc0, mw=mw: self.seq(
                    "dve", e, [
                        lambda v: v.tensor_mul(self.SA[:, msl], self.SA[:, msl],
                                               PLOC[:, mc0:128]),
                        lambda v: v.tensor_add(self.SA[:, msl], self.SA[:, msl],
                                               PFUT[:, mc0:128]),
                    ]))
                for rd in range(4):
                    co = 32 * tt + 8 * rd
                    self.blk(vector=lambda e, Wp=Wp, co=co: self.seq("dve", e, [
                        lambda v: v.max(mx8[:, :], self.SA[:, :Wp]),
                        lambda v: v.max_index(u8[:, :], mx8[:, :], self.SA[:, :Wp]),
                        lambda v: v.match_replace(self.SA[:, :Wp], mx8[:, :],
                                                 self.SA[:, :Wp], NEG_REP),
                        lambda v: v.tensor_copy(SIGF[:, co:co + 8], u8[:, :]),
                        lambda v: v.tensor_copy(EMK[:, co:co + 8], mx8[:, :]),
                    ]))
                ts32 = slice(32 * tt, 32 * (tt + 1))
                self.blk(vector=lambda e, ts32=ts32: self.seq("dve", e, [
                    lambda v: v.tensor_scalar(EMK[:, ts32], EMK[:, ts32],
                                              -5e37, None, ALU.is_gt),
                    lambda v: v.tensor_scalar(EMK[:, ts32], EMK[:, ts32],
                                              1e30, -1e30, ALU.mult, ALU.add),
                    lambda v: v.tensor_copy(SIGI[:, ts32], SIGF[:, ts32]),
                    lambda v: v.tensor_add(self.SB_[:, 0:32], SIGF[:, ts32],
                                           RTPR[:, :]),
                    lambda v: v.tensor_copy(SIGP[:, ts32], self.SB_[:, 0:32]),
                ]))
            if dbg == 2:
                return
            def idxbuild(e, qn, SRC, tgt, split16):
                for g in range(8):
                    for tp in range(NT):
                        if not split16:
                            self.dma(qn, e,
                                     bass.AP(tgt.tensor, 256 * tp + g,
                                             [[PADR, 16], [8, 32]]),
                                     bass.AP(SRC, (16 * g) * (NT * 32) + 32 * tp,
                                             [[NT * 32, 16], [1, 32]]),
                                     allow_slow_non_contiguous=True)
                        else:
                            for hf in range(2):
                                self.dma(qn, e,
                                         bass.AP(tgt.tensor,
                                                 256 * tp + 128 * hf + g,
                                                 [[PADR, 16], [8, 16]]),
                                         bass.AP(SRC, (16 * g) * (NT * 32)
                                                 + 32 * tp + 16 * hf,
                                                 [[NT * 32, 16], [1, 16]]),
                                         allow_slow_non_contiguous=True)
            self.blk(sync=lambda e: idxbuild(e, "hw", SIGI, d_i1, False),
                     scalar=lambda e: idxbuild(e, "hw", SIGP, d_i2, True))
            IDX1 = self.sb("IDX1", [128, NT * 256], I16)
            IDX2 = self.sb("IDX2", [128, NT * 256], I16)
            self.blk(gpsimd=lambda e: (self.dwait(e), [self.dma(
                "sw", e, XX[:, :],
                bass.AP(DD.tensor, 0, [[0, 8], [PADR, 16], [1, NT * 256]]))
                for XX, DD in [(IDX1, d_i1), (IDX2, d_i2)]]))
            self.blk(vector=lambda e: self.dwait(e))

            if dbg == 3:
                return
            qcT = self.sb("qcT", [128, max(1, H1 // 2), 128], BF16)
            qXT = {"A": self.sb("qAT", [32, H1, 128], BF16),
                   "B": self.sb("qBT", [32, H1, 128], BF16)}
            qctm = self.sb("qctm", [128, H1 * DH1], BF16)
            cqT = self.sb("cqT", [128, CQC, 128], BF16)
            QRC = self.sb("QRC", [128, 32], F32)
            ur = self.sb("ur", [128, H1 * 32], F32)
            QTM = self.sb("QTM", [128, H1 * 32], F32)
            QB16 = self.sb("QB16", [128, H1 * 32], BF16)
            QATMB = self.sb("QATMB", [128, H1 * 32], BF16)
            KFG = self.sb("KFG", [128, 32 * 128], BF16)
            MRG = self.sb("MRG", [128, 16 * 128], BF16)
            CONT = self.sb("CONT", [128, H1 * 32], F32)
            PRB = self.sb("PRB", [128, 2, H1 * 224], BF16)
            ZCOL = self.sb("ZCOL", [128, 2, H1], F32)
            rmax = self.sb("rmax", [128, 4], F32)
            zc = self.sb("zc", [128, 2], F32)
            PD = self.sb("PD", [128, T], BF16)
            OCT = self.sb("OCT", [128, KVC, 128], BF16)
            ATT = self.sb("ATT", [128, KC, 128], BF16)
            ES = self.sb("ES", [128, 32], F32)
            ROPE = self.sb("ROPE", [128, H1 * 32], F32)
            KW = self.sb("KW", [128, KC, 192], BF16)
            EPROD = self.sb("EPROD", [128, 2048], BF16)
            self.blk(gpsimd=lambda e: self.lib(e, LC.mlp))

            def score_phase(tt):
                par = tt % 2
                tsl = slice(128 * tt, 128 * (tt + 1))
                Wd = 128 * (tt + 1)
                w0 = max(0, 128 * tt - 64)
                wN = min(192, Wd - w0)
                self.blk(gpsimd=lambda e: ([self.dma(
                    "sw", e, HHc[:, kc, :128],
                    self.d_hh[128 * kc:128 * (kc + 1), tsl])
                    for kc in range(DC)],
                    [self.dma("sw", e, KW[:, oc_, :wN],
                              d_kfT[128 * oc_:128 * (oc_ + 1), w0:w0 + wN])
                     for oc_ in range(KC)]))
                def cq_pf(t, par, oc):
                    if oc == 0:
                        self.dwait(t)
                    ps = self.PSA if par == 0 else self.PSB
                    for kc in range(DC):
                        mm = t.matmul(ps[:, :],
                                      Wdq[:, kc, 128 * oc:128 * (oc + 1)],
                                      HHc[:, kc, :128], start=(kc == 0),
                                      stop=(kc == DC - 1))
                    return mm

                def cq_cf(v, par, oc):
                    ps = self.PSA if par == 0 else self.PSB
                    return v.tensor_copy(cqT[:, oc, :], ps[:, :])
                self.pipe2([(lambda t, par, oc=oc: cq_pf(t, par, oc),
                             lambda v, par, oc=oc: cq_cf(v, par, oc))
                            for oc in range(CQC)])

                def qc_pf(t, par, h):
                    hb = (h % 2) * 64
                    for kc in range(CQC):
                        mm = t.matmul(self.PSA[hb:hb + 64, :],
                                      Wuq[:, kc, 64 * h:64 * (h + 1)],
                                      cqT[:, kc, :], start=(kc == 0),
                                      stop=(kc == CQC - 1))
                    return mm

                def qc_cf(v, par, h):
                    hb = (h % 2) * 64
                    return v.tensor_copy(qcT[hb:hb + 64, h // 2, :],
                                         self.PSA[hb:hb + 64, :])
                self.pipe2([(lambda t, par, h=h: qc_pf(t, par, h),
                             lambda v, par, h=h: qc_cf(v, par, h))
                            for h in range(H1)])

                def qm_pf(t, par, oc):
                    ps = self.PSA if par == 0 else self.PSB
                    for kc in range(CQC):
                        mm = t.matmul(ps[:, :], cqT[:, kc, :],
                                      Wuq[:, kc, 128 * oc:128 * (oc + 1)],
                                      start=(kc == 0), stop=(kc == CQC - 1))
                    return mm

                def qm_cf(v, par, oc):
                    ps = self.PSA if par == 0 else self.PSB
                    return v.tensor_copy(qctm[:, 128 * oc:128 * (oc + 1)],
                                         ps[:, :])
                self.pipe2([(lambda t, par, oc=oc: qm_pf(t, par, oc),
                             lambda v, par, oc=oc: qm_cf(v, par, oc))
                            for oc in range(H1 * DH1 // 128)])
                for oc in range(H1 * DR // 128):
                    self.blk(tensor=lambda e, oc=oc: [e.matmul(
                        self.PSA[:, :], cqT[:, kc, :],
                        Wqr[:, kc, 128 * oc:128 * (oc + 1)],
                        start=(kc == 0), stop=(kc == CQC - 1))
                        for kc in range(CQC)])
                    self.blk(scalar=lambda e, oc=oc: self.seq("act", e, [
                        lambda s: s.activation(self.SA[:, :128], self.PSA[:, :],
                                               AF.Abs),
                        lambda s: s.activation(self.SA[:, :128], self.SA[:, :128],
                                               AF.Exp, scale=-1.0),
                    ]))
                    self.blk(vector=lambda e: e.tensor_scalar_add(
                        self.SA[:, :128], self.SA[:, :128], 1.0))
                    self.blk(scalar=lambda e: e.activation(
                        self.SA[:, :128], self.SA[:, :128], AF.Ln))
                    self.blk(vector=lambda e, oc=oc: e.scalar_tensor_tensor(
                        ur[:, 128 * oc:128 * (oc + 1)], self.PSA[:, :], 0.0,
                        self.SA[:, :128], ALU.max, ALU.add))
                for ver in ("A", "B"):
                    QQ = qXT[ver]
                    self.blk(gpsimd=lambda e, ver=ver: self.dma(
                        "sw", e, QRC[:, :],
                        self.CN["qrot" + ver][tsl, :]))
                    cs_ap = bass.AP(QRC, 0, [[32, 128], [0, H1], [1, 16]])
                    sn_ap = bass.AP(QRC, 16, [[32, 128], [0, H1], [1, 16]])
                    rr = lambda t: t.rearrange("p (h x d) -> p h x d", h=H1, x=2)
                    u1, u2 = rr(ur[:])[:, :, 0, :], rr(ur[:])[:, :, 1, :]
                    S1 = self.SA[:, :H1 * 16].rearrange("p (h d) -> p h d", h=H1)
                    S2 = self.SB_[:, :H1 * 16].rearrange("p (h d) -> p h d", h=H1)
                    d1, d2 = rr(QTM[:])[:, :, 0, :], rr(QTM[:])[:, :, 1, :]
                    self.blk(vector=lambda e: self.dwait(e))
                    self.blk(vector=lambda e, u1=u1, u2=u2, S1=S1, S2=S2,
                             d1=d1, d2=d2, cs_ap=cs_ap, sn_ap=sn_ap:
                             self.seq("dve", e, [
                        lambda v: v.tensor_tensor(S1, u1, cs_ap, ALU.mult),
                        lambda v: v.tensor_tensor(S2, u2, sn_ap, ALU.mult),
                        lambda v: v.tensor_sub(S1, S1, S2),
                        lambda v: v.tensor_copy(d1, S1),
                        lambda v: v.tensor_tensor(S1, u1, sn_ap, ALU.mult),
                        lambda v: v.tensor_tensor(S2, u2, cs_ap, ALU.mult),
                        lambda v: v.tensor_add(S1, S1, S2),
                        lambda v: v.tensor_copy(d2, S1),
                        lambda v: v.tensor_copy(QB16[:, :], QTM[:, :]),
                    ]))
                    if ver == "A":
                        self.blk(vector=lambda e: e.tensor_copy(
                            QATMB[:, :], QB16[:, :]))

                    def rt_pf(t, par, h):
                        return t.transpose(
                            self.PSTB[64 * par:64 * par + 32, :],
                            QB16[:, 32 * h:32 * (h + 1)], self.IDB[:, :])

                    def rt_cf(v, par, h, QQ):
                        return v.tensor_copy(
                            QQ[:, h, :], self.PSTB[64 * par:64 * par + 32, :])
                    self.pipe2([(lambda t, par, h=h: rt_pf(t, par, h),
                                 lambda v, par, h=h, QQ=QQ: rt_cf(v, par, h,
                                                                  QQ))
                                for h in range(H1)])
                if dbg == 5:
                    return
                i1sl = IDX1[:, 256 * tt:256 * (tt + 1)]
                i2a = IDX2[:, 256 * tt:256 * tt + 128]
                i2b = IDX2[:, 256 * tt + 128:256 * (tt + 1)]

                for pr in range(H1 // 2):
                    def gkf(e, pr=pr):
                        for j in range(4):
                            self._scnt["sw"] += 16
                            e.dma_gather(
                                KFG[:, 1024 * j:1024 * (j + 1)].rearrange(
                                    "p (s d) -> p s d", d=128),
                                d_kf[pr],
                                IDX1[:, 256 * tt + 64 * j:
                                     256 * tt + 64 * (j + 1)],
                                1024, self.greg(e, 1024),
                                128, elem_step=128,
                            ).then_inc(self._sem["sw"], 16)
                    self.blk(gpsimd=gkf)
                    for j in range(2):
                        h_ = 2 * pr + j
                        self.blk(vector=lambda e, pr=pr, j=j, h_=h_: (
                            self.dwait(e), self.seq("dve", e, [
                            lambda v: v.tensor_tensor(
                                EPROD[:, :2048],
                                KFG[:].rearrange("p (s d) -> p s d",
                                                 d=128)[:, :, 64 * j:64 * (j + 1)],
                                bass.AP(qctm, 128 * pr + 64 * j,
                                        [[H1 * DH1, 128], [0, 32], [1, 64]]),
                                ALU.mult),
                            lambda v: v.tensor_reduce(
                                CONT[:, 32 * h_:32 * (h_ + 1)],
                                EPROD[:, :2048].rearrange(
                                    "p (r d) -> p r d", d=64),
                                AX.X, ALU.add),
                            ])))
                if dbg in (6, 8, 9):
                    return
                for hf in range(2):
                    def gmr(e, hf=hf):
                        i2o = 256 * tt + 128 * hf
                        for j in range(2):
                            self._scnt["sw"] += 16
                            e.dma_gather(
                                MRG[:, 1024 * j:1024 * (j + 1)].rearrange(
                                    "p (s d) -> p s d", d=128),
                                (d_prA if hf == 0 else d_prB)[:, :],
                                IDX2[:, i2o + 64 * j:i2o + 64 * (j + 1)],
                                1024, self.greg(e, 1024), 128,
                                elem_step=128).then_inc(self._sem["sw"], 16)
                    self.blk(gpsimd=gmr)
                    self.blk(vector=lambda e: self.dwait(e))
                    for h in range(H1):
                        self.blk(vector=lambda e, h=h, hf=hf: self.seq("dve", e, [
                            lambda v: v.tensor_tensor(
                                EPROD[:, 512 * hf:512 * (hf + 1)],
                                MRG[:].rearrange("p (s d) -> p s d",
                                                 d=128)[:, :, 0:32],
                                bass.AP(QATMB, 32 * h, [[H1 * 32, 128],
                                                        [0, 16], [1, 32]]),
                                ALU.mult),
                            lambda v: v.tensor_reduce(
                                ROPE[:, 32 * h + 16 * hf:32 * h + 16 * (hf + 1)],
                                EPROD[:, 512 * hf:512 * (hf + 1)].rearrange(
                                    "p (r d) -> p r d", d=32), AX.X, ALU.add),
                        ]))
                if dbg == 7:
                    return
                for h in range(H1):
                    hc, hr = h // hp1, (h % hp1) * DH1
                    def tband(e, h=h, hc=hc, hr=hr, w0=w0, wN=wN):
                        e.matmul(self.PS0[:, :wN], qcT[hr:hr + DH1, h // 2, :],
                                 KW[hr:hr + DH1, hc, :wN],
                                 start=True, stop=False)
                        e.matmul(self.PS0[:, :wN], qXT["B"][:, h, :],
                                 mB[:, w0:w0 + wN], start=False, stop=True)
                        if tt == 0:
                            e.matmul(self.PS1[:, :wN], qcT[hr:hr + DH1, h // 2, :],
                                     KW[hr:hr + DH1, hc, :wN],
                                     start=True, stop=False)
                            e.matmul(self.PS1[:, :wN], qXT["A"][:, h, :],
                                     mA[:, :wN], start=False, stop=True)
                    self.blk(tensor=tband)
                    MB_ = MBAND0 if tt == 0 else MBAND
                    wsl = slice(0, wN)
                    def vsm(e, h=h, wsl=wsl, wN=wN, par=par):
                        self.seq("dve", e, [
                            lambda v: v.tensor_copy(
                                ES[:, :], ROPE[:, 32 * h:32 * (h + 1)]),
                            lambda v: v.tensor_add(
                                ES[:, :], ES[:, :],
                                CONT[:, 32 * h:32 * (h + 1)]),
                            lambda v: v.tensor_add(
                                ES[:, :], ES[:, :],
                                EMK[:, 32 * tt:32 * (tt + 1)]),
                            lambda v: v.tensor_tensor(
                                self.SA[:, wsl], self.PS0[:, :wN],
                                MB_[:, wsl], ALU.add),
                        ] + ([
                            lambda v: v.tensor_tensor(
                                self.SB_[:, wsl], self.PS1[:, :wN],
                                MABS0[:, wsl], ALU.add),
                            lambda v: v.tensor_max(
                                self.SA[:, wsl], self.SA[:, wsl],
                                self.SB_[:, wsl]),
                        ] if tt == 0 else []) + [
                            lambda v: v.tensor_reduce(
                                rmax[:, 0:1], self.SA[:, wsl], AX.X, ALU.max),
                            lambda v: v.tensor_reduce(
                                rmax[:, 1:2], ES[:, :], AX.X, ALU.max),
                            lambda v: v.tensor_max(
                                rmax[:, 0:1], rmax[:, 0:1], rmax[:, 1:2]),
                            lambda v: v.tensor_scalar(
                                rmax[:, 2:3], rmax[:, 0:1], -scale1, None,
                                ALU.mult),
                        ])
                    self.blk(vector=vsm)
                    def sexp(e, h=h, wN=wN, par=par):
                        self.seq("act", e, [
                            lambda s: s.activation(
                                PRB[:, par, 224 * h:224 * h + wN],
                                self.SA[:, 0:wN], AF.Exp,
                                bias=rmax[:, 2:3], scale=scale1,
                                accum_out=zc[:, 0:1]),
                            lambda s: s.activation(
                                PRB[:, par, 224 * h + 192:224 * h + 224],
                                ES[:, :], AF.Exp,
                                bias=rmax[:, 2:3], scale=scale1,
                                accum_out=zc[:, 1:2]),
                        ])
                    self.blk(scalar=sexp)
                    self.blk(vector=lambda e, h=h, par=par: self.seq("dve", e, [
                        lambda v: v.tensor_add(zc[:, 0:1], zc[:, 0:1],
                                               zc[:, 1:2]),
                        lambda v: v.reciprocal(ZCOL[:, par, h:h + 1],
                                               zc[:, 0:1]),
                    ]))

            def pv_phase(tt):
                par = tt % 2
                tsl = slice(128 * tt, 128 * (tt + 1))
                Wd = 128 * (tt + 1)
                w0 = max(0, 128 * tt - 64)
                wN = min(192, Wd - w0)
                WdS = 128 if tt == 0 else 128 * tt + 64
                for h in range(H1):
                    def gsc(e, h=h, par=par, WdS=WdS):
                        self.lib(e, LC.local_scatter)
                        e.local_scatter(PD[:, :WdS],
                                        PRB[:, par, 224 * h + 192:224 * h + 224],
                                        SIGI[:, 32 * tt:32 * (tt + 1)],
                                        128, WdS, 32)
                    self.blk(gpsimd=gsc)
                    aN = WdS - w0 if tt > 0 else wN
                    self.blk(vector=lambda e, h=h, par=par, w0=w0, wN=wN,
                             aN=aN, WdS=WdS: self.seq("dve", e, [
                                 lambda v: v.tensor_add(
                                     PD[:, w0:w0 + aN], PD[:, w0:w0 + aN],
                                     PRB[:, par, 224 * h:224 * h + aN])] + ([
                                 lambda v: v.tensor_copy(
                                     PD[:, WdS:w0 + wN],
                                     PRB[:, par, 224 * h + aN:224 * h + wN])]
                                 if wN > aN else [])))
                    nkc = Wd // 128
                    evT = self.resv("pe", nkc)
                    evC = self.resv("dve", nkc)
                    evPV = self.resv("pe2", nkc)
                    PTb2 = [self.SAB16[:, 0:128], self.SAB16[:, 128:256]]

                    with self.nc.Block() as block:
                        @block.tensor
                        def _(t, nkc=nkc, evT=evT, evC=evC, evPV=evPV):
                            self.fire(t.transpose(
                                self.PSTB[:, :], PD[:, 0:128],
                                self.IDB[:, :]), evT[0])
                            for ci in range(nkc):
                                self.wait(t, evC[ci])
                                self.fire(t.matmul(
                                    self.PS0[:, :DCKV], PTb2[ci % 2],
                                    ckvtm[:, ci, :], start=(ci == 0),
                                    stop=(ci == nkc - 1)), evPV[ci])
                                if ci + 1 < nkc:
                                    self.fire(t.transpose(
                                        self.PSTB[:, :],
                                        PD[:, 128 * (ci + 1):128 * (ci + 2)],
                                        self.IDB[:, :]), evT[ci + 1])

                        @block.vector
                        def _(v, nkc=nkc, evT=evT, evC=evC, evPV=evPV):
                            for ci in range(nkc):
                                self.wait(v, evT[ci])
                                if ci >= 2:
                                    self.wait(v, evPV[ci - 2])
                                self.fire(v.tensor_copy(
                                    PTb2[ci % 2], self.PSTB[:, :]),
                                    evC[ci])
                    self.blk(vector=lambda e, h=h, par=par: e.tensor_scalar(
                        self.SA[:, :DCKV], self.PS0[:, :DCKV],
                        ZCOL[:, par, h:h + 1], None, ALU.mult))
                    for kvc in range(KVC):
                        self.blk(tensor=lambda e, kvc=kvc: e.transpose(
                            self.PSTF[:, :],
                            self.SA[:, 128 * kvc:128 * (kvc + 1)],
                            self.IDF[:, :]))
                        self.blk(vector=lambda e, kvc=kvc: e.tensor_copy(
                            OCT[:, kvc, :], self.PSTF[:, :]))
                    def tuv(e, h=h):
                        for kvc in range(KVC):
                            e.matmul(self.PSD[(h % hp1) * DH1:
                                              (h % hp1) * DH1 + DH1, :],
                                     Wuv[:, kvc, DH1 * h:DH1 * (h + 1)],
                                     OCT[:, kvc, :], start=(kvc == 0),
                                     stop=(kvc == KVC - 1))
                    self.blk(tensor=tuv)
                    if h % hp1 == hp1 - 1:
                        self.blk(vector=lambda e, h=h: e.tensor_copy(
                            ATT[:, h // hp1, :], self.PSD[:, :]))
                for oc in range(DC):
                    self.blk(tensor=lambda e, oc=oc: [e.matmul(
                        self.PSA[:, :], WoutB[:, kc, 128 * oc:128 * (oc + 1)],
                        ATT[:, kc, :], start=(kc == 0), stop=(kc == KC - 1))
                        for kc in range(KC)])
                    self.blk(vector=lambda e, oc=oc: e.scalar_tensor_tensor(
                        self.X[:, oc, tsl], self.PSA[:, :],
                        self.MOD["ga"][:, oc:oc + 1], self.X[:, oc, tsl],
                        ALU.mult, ALU.add))

            for tp in range(0, NT, 2):
                score_phase(tp)
                if dbg == 4:
                    continue
                if tp + 1 < NT:
                    score_phase(tp + 1)
                self.blk(gpsimd=lambda e: self.lib(e, LC.local_scatter))
                pv_phase(tp)
                if tp + 1 < NT:
                    pv_phase(tp + 1)
                self.blk(gpsimd=lambda e: self.lib(e, LC.mlp))

    def stage_l1_ffn(self):
        T, D, DC, NT, E = self.T, self.D, self.DC, self.NT, self.E
        HID, RH = self.HID, self.RH
        TC4 = T // self.TCW
        W = self.TCW
        nks = HID // 128
        nkr = E * RH // 128
        nkt = nks + nkr
        with self.scope():
            d_gates = self.dint("d_gates", [E, T])
            d_w3bf = self.dint("d_w3bf", [HID + E * RH, D], BF16)
            CST = self.sb("CST", [128, D], BF16)

            def gcast(e):
                for k in range(nks):
                    self.dma("sw", e, CST[:, :],
                             self.i_sw3T[128 * k:128 * (k + 1), :])
                    self.dwait(e, ("sw",))
                    self.dma("sw", e, d_w3bf[128 * k:128 * (k + 1), :], CST[:, :])
                    self.dwait(e, ("sw",))
                for ex in range(E):
                    for k in range(RH // 128):
                        kk = nks + ex * (RH // 128) + k
                        self.dma("sw", e, CST[:, :],
                                 self.i_w3T[ex, 128 * k:128 * (k + 1), :])
                        self.dwait(e, ("sw",))
                        self.dma("sw", e, d_w3bf[128 * kk:128 * (kk + 1), :],
                                 CST[:, :])
                        self.dwait(e, ("sw",))
            self.blk(gpsimd=gcast)
            RW = self.sb("RW", [128, DC, E], F32)
            GT = self.sb("GT", [8, T], F32)
            onepc = self.sb("onepc2", [128, DC], F32)
            self.blk(gpsimd=lambda e: [self.dma(
                "sw", e, RW[:, kc, :], self.i_routerT[128 * kc:128 * (kc + 1), :])
                for kc in range(DC)])
            self.blk(vector=lambda e: (self.dwait(e), e.tensor_scalar_add(
                onepc[:], self.MOD["cb"][:], 1.0)))
            for tc in range(TC4):
                sl = slice(W * tc, W * (tc + 1))
                for kc in range(DC):
                    self.blk(vector=lambda e, sl=sl, kc=kc: self.seq("dve", e, [
                        lambda v: v.tensor_mul(self.SA[:, :W], self.X[:, kc, sl],
                                               self.ROWRK[:, sl]),
                        lambda v: v.tensor_sub(self.SA[:, :W], self.SA[:, :W],
                                               self.ROWWK[:, sl]),
                        lambda v: v.tensor_scalar(self.SA[:, :W], self.SA[:, :W],
                                                  onepc[:, kc:kc + 1],
                                                  self.MOD["sh"][:, kc:kc + 1],
                                                  ALU.mult, ALU.add),
                    ]))
                    self.blk(tensor=lambda e, kc=kc: e.matmul(
                        self.PS1[0:8, :W], RW[:, kc, :], self.SA[:, :W],
                        start=(kc == 0), stop=(kc == DC - 1)))
                self.blk(vector=lambda e, sl=sl: e.tensor_copy(
                    GT[0:8, sl], self.PS1[0:8, :W]))
            self.blk(scalar=lambda e: e.activation(
                GT[0:8, :], GT[0:8, :], AF.Sigmoid))
            SCT = self.sb("SCT", [128, 8], F32)
            u8 = self.sb("u8f", [128, 8], U32)
            i0c = self.sb("i0c", [128, 2], F32)
            IOT8 = self.sb("IOT8", [128, 8], F32)
            EB = self.sb("EB", [128, 8], F32)
            MSK = self.sb("MSK", [128, 8], BF16)
            MSKT = self.sb("MSKT", [8, T], F32)
            self.blk(gpsimd=lambda e: (
                self.lib(e, LC.standard),
                e.iota(IOT8[:, :], [[1, 8]], channel_multiplier=0,
                       allow_small_or_imprecise_dtypes=True),
                self.dma("sw", e, EB[:, :],
                         bass.AP(self.i_expbias.tensor, 0, [[0, 128], [1, 8]]))))
            self.blk(vector=lambda e: self.dwait(e))
            for tt in range(NT):
                tsl = slice(128 * tt, 128 * (tt + 1))
                self.blk(tensor=lambda e, tsl=tsl: e.transpose(
                    self.PSTF[:, 0:8], GT[0:8, tsl], self.IDF[0:8, 0:8]))
                self.blk(vector=lambda e: e.tensor_tensor(
                    SCT[:, :], self.PSTF[:, 0:8], EB[:, :], ALU.add))
                self.blk(vector=lambda e: self.seq("dve", e, [
                    lambda v: v.max(self.SA[:, 0:8], SCT[:, :]),
                    lambda v: v.max_index(u8[:, :], self.SA[:, 0:8], SCT[:, :]),
                    lambda v: v.tensor_copy(i0c[:, :], u8[:, 0:2]),
                    lambda v: v.tensor_scalar(MSK[:, :], IOT8[:, :],
                                              i0c[:, 0:1], None, ALU.is_equal),
                    lambda v: v.tensor_scalar(self.SB_[:, 0:8], IOT8[:, :],
                                              i0c[:, 1:2], None, ALU.is_equal),
                    lambda v: v.tensor_add(MSK[:, :], MSK[:, :],
                                           self.SB_[:, 0:8]),
                ]))
                self.blk(tensor=lambda e, tt=tt: e.transpose(
                    self.PSTB[0:8, :], MSK[:, :], self.IDB[:, :]))
                self.blk(vector=lambda e, tsl=tsl: e.tensor_copy(
                    MSKT[0:8, tsl], self.PSTB[0:8, :]))
            self.blk(vector=lambda e: e.tensor_mul(GT[0:8, :], GT[0:8, :],
                                                   MSKT[0:8, :]))
            self.blk(sync=lambda e: self.dma("hw", e, d_gates[0:8, :],
                                             GT[0:8, :]))
            self.blk(vector=lambda e: self.dwait(e, ("hw",)))

            self.blk(sync=lambda e: [self.dma(
                "hw", e, self.d_xs[128 * c:128 * (c + 1), :], self.X[:, c, :])
                for c in range(DC)])
            self.blk(vector=lambda e: self.dwait(e, ("hw",)))

            HHr = self.sb("HHr", [128, DC, T], BF16)
            self.blk(gpsimd=lambda e: [self.dma(
                "sw", e, HHr[:, kc, :], self.d_hh[128 * kc:128 * (kc + 1), :])
                for kc in range(DC)])
            W1B = self.sb("W1B", [128, DC, 128], BF16)
            W2B = self.sb("W2B", [128, DC, 128], BF16)
            GATE = self.sb("GATE", [128, T], BF16)
            HP = self.sb("HP", [128, 2, T], BF16)
            self.blk(vector=lambda e: self.dwait(e))
            hwm = [0, 0]

            def pass1(w1_ap, c1off, w2_ap, c2off, tgt_ap, nhc, gated):
                for hc in range(nhc):
                    par = hc % 2
                    self.blk(gpsimd=lambda e, hc=hc: (
                        self.wload(e, W1B, w1_ap, c1off + 128 * hc, 128, DC),
                        self.wload(e, W2B, w2_ap, c2off + 128 * hc, 128, DC)))
                    evM = self.resv("pe", TC4)
                    evS = self.resv("act", TC4)
                    nv = 2 if gated else 1
                    evV = self.resv("dve", nv * TC4)

                    with self.nc.Block() as block:
                        @block.tensor
                        def _(t, evM=evM, evV=evV):
                            for tc in range(TC4):
                                sl = slice(W * tc, W * (tc + 1))
                                if tc == 0:
                                    self.dwait(t)
                                else:
                                    self.wait(t, evV[nv * tc - 1])
                                for kc in range(DC):
                                    t.matmul(self.PS0[:, :W], W1B[:, kc, :],
                                             HHr[:, kc, sl], start=(kc == 0),
                                             stop=(kc == DC - 1))
                                for kc in range(DC):
                                    mm = t.matmul(self.PS1[:, :W],
                                                  W2B[:, kc, :],
                                                  HHr[:, kc, sl],
                                                  start=(kc == 0),
                                                  stop=(kc == DC - 1))
                                self.fire(mm, evM[tc])

                        @block.scalar
                        def _(s, evM=evM, evS=evS, evV=evV):
                            for tc in range(TC4):
                                self.wait(s, evM[tc])
                                if tc >= 1:
                                    self.wait(s, evV[nv * tc - 1])
                                self.fire(s.activation(
                                    self.SA[:, :W], self.PS0[:, :W],
                                    AF.Silu), evS[tc])

                        @block.vector
                        def _(v, par=par, evS=evS, evV=evV):
                            for tc in range(TC4):
                                sl = slice(W * tc, W * (tc + 1))
                                self.wait(v, evS[tc])
                                if tc == 0 and hwm[par]:
                                    self.dwait(v, ("hw",))
                                if gated:
                                    self.fire(v.tensor_tensor(
                                        self.SB_[:, :W], self.SA[:, :W],
                                        self.PS1[:, :W], ALU.mult),
                                        evV[2 * tc])
                                    self.wait(v, evV[2 * tc])
                                    self.fire(v.tensor_mul(
                                        HP[:, par, sl], self.SB_[:, :W],
                                        GATE[:, sl]), evV[2 * tc + 1])
                                else:
                                    self.fire(v.tensor_tensor(
                                        HP[:, par, sl], self.SA[:, :W],
                                        self.PS1[:, :W], ALU.mult),
                                        evV[tc])
                    self.blk(sync=lambda e, hc=hc, par=par: self.dma(
                        "hw", e,
                        bass.AP(tgt_ap[0], tgt_ap[1] + 128 * hc * T,
                                [[T, 128], [1, T]]),
                        HP[:, par, :]))
                    hwm[par] = self._scnt["hw"]
            pass1(self.i_sw1T, 0, self.i_sw2T, 0, (self.d_g.tensor, 0),
                  nks, False)
            for ex in range(E):
                self.blk(gpsimd=lambda e, ex=ex: self.dma(
                    "sw", e, GATE[:, :],
                    bass.AP(d_gates.tensor, ex * T, [[0, 128], [1, T]])))
                self.blk(vector=lambda e: self.dwait(e))
                pass1(self.i_w12T[ex], 0, self.i_w12T[ex], RH,
                      (self.d_ge.tensor, ex * RH * T), RH // 128, True)
            self.blk(vector=lambda e: self.dwait(e, ("hw",)))

            KCH = 24
            nch = (nkt + KCH - 1) // KCH
            XTD = self.sb("XTD", [128, DC, W], F32)
            PANC = self.sb("PANC", [128, KCH, W], BF16)
            W3CC = self.sb("W3CC", [128, KCH, 128], BF16)
            for tc in range(TC4):
                sl = slice(W * tc, W * (tc + 1))
                self.blk(sync=lambda e, sl=sl: [self.dma(
                    "hw", e, XTD[:, dc, :],
                    self.d_xs[128 * dc:128 * (dc + 1), sl])
                    for dc in range(DC)])
                self.blk(vector=lambda e: self.dwait(e, ("hw",)))
                for ch in range(nch):
                    k0 = KCH * ch
                    kn = min(KCH, nkt - k0)

                    def ldp(e, k0=k0, kn=kn, sl=sl):
                        for j in range(kn):
                            k = k0 + j
                            if k < nks:
                                src = bass.AP(self.d_g.tensor,
                                              128 * k * T + sl.start,
                                              [[T, 128], [1, W]])
                            else:
                                src = bass.AP(self.d_ge.tensor,
                                              128 * (k - nks) * T + sl.start,
                                              [[T, 128], [1, W]])
                            self.dma("sw", e, PANC[:, j, :], src)
                    self.blk(gpsimd=ldp)
                    for dc in range(DC):
                        self.blk(gpsimd=lambda e, k0=k0, kn=kn, dc=dc: [
                            self.dma("sw", e, W3CC[:, j, :],
                                     d_w3bf[128 * (k0 + j):128 * (k0 + j + 1),
                                            128 * dc:128 * (dc + 1)])
                            for j in range(kn)])

                        def t2(e, kn=kn):
                            self.dwait(e)
                            for j in range(kn):
                                e.matmul(self.PS0[:, :W], W3CC[:, j, :],
                                         PANC[:, j, :], start=(j == 0),
                                         stop=(j == kn - 1))
                        self.blk(tensor=t2)
                        self.blk(vector=lambda e, dc=dc: e.scalar_tensor_tensor(
                            XTD[:, dc, :], self.PS0[:, :W],
                            self.MOD["gb"][:, dc:dc + 1],
                            XTD[:, dc, :], ALU.mult, ALU.add))
                self.blk(sync=lambda e, sl=sl: [self.dma(
                    "hw", e, self.o_outT[128 * dc:128 * (dc + 1), sl],
                    XTD[:, dc, :]) for dc in range(DC)])
                self.blk(vector=lambda e: self.dwait(e, ("hw",)))

    def stage_out(self):
        def g(e):
            for c in range(self.DC):
                self.dma("hw", e, self.o_outT[128 * c:128 * (c + 1), :],
                         self.X[:, c, :])
            self.dwait(e, ("hw",))
        self.blk(sync=g)

    def finish(self):
        self.es.close()
        self.nc.finalize()
        mybir.codegen_inst_isa_subclasses(self.nc)
        return self.nc


def _prep_core_inputs(cfg, inp, b):
    f32 = np.float32
    m = {}
    m["xT"] = np.ascontiguousarray(np.asarray(inp["L0_attn_x"][b], f32).T)
    m["wqkvT"] = np.ascontiguousarray(np.asarray(inp["L0_attn_w_qkv"], f32).T)
    m["wprojT"] = np.ascontiguousarray(np.asarray(inp["L0_attn_w_proj"], f32).T)
    m["Wr"] = np.asarray(inp["L0_ffn_Wr"], f32)
    m["br"] = np.asarray(inp["L0_ffn_br"], f32)[None, :]
    m["W1"] = np.asarray(inp["L0_ffn_W1"], f32)
    m["b1"] = np.asarray(inp["L0_ffn_b1"], f32)
    m["W2"] = np.asarray(inp["L0_ffn_W2"], f32)
    m["b2"] = np.asarray(inp["L0_ffn_b2"], f32)
    m["c0"] = np.asarray(inp["L0_c"][b], f32)[None, :]
    m["wmod0"] = np.asarray(inp["L0_wmod"], f32)
    m["Wdkv"] = np.asarray(inp["L1_attn_W_dkv"], f32)
    m["Wuk"] = np.asarray(inp["L1_attn_W_uk"], f32)
    m["WukT"] = np.ascontiguousarray(np.asarray(inp["L1_attn_W_uk"], f32).T)
    m["Wuv"] = np.asarray(inp["L1_attn_W_uv"], f32)
    m["Wdq"] = np.asarray(inp["L1_attn_W_dq"], f32)
    m["Wuq"] = np.asarray(inp["L1_attn_W_uq"], f32)
    m["Wqr"] = np.asarray(inp["L1_attn_W_qr"], f32)
    m["Wkr"] = np.asarray(inp["L1_attn_W_kr"], f32)
    m["Wout"] = np.asarray(inp["L1_attn_W_out"], f32)
    m["idxwq"] = np.asarray(inp["L1_attn_idx_wq"], f32)
    m["idxwk"] = np.asarray(inp["L1_attn_idx_wk"], f32)
    m["idxw"] = np.asarray(inp["L1_attn_idx_w"], f32)[None, :]
    m["w12T"] = np.ascontiguousarray(
        np.asarray(inp["L1_ffn_w12"], f32).transpose(0, 2, 1))
    m["w3T"] = np.ascontiguousarray(
        np.asarray(inp["L1_ffn_w3"], f32).transpose(0, 2, 1))
    m["sw1T"] = np.ascontiguousarray(np.asarray(inp["L1_ffn_shared_w1"], f32).T)
    m["sw2T"] = np.ascontiguousarray(np.asarray(inp["L1_ffn_shared_w2"], f32).T)
    m["sw3T"] = np.ascontiguousarray(np.asarray(inp["L1_ffn_shared_w3"], f32).T)
    m["routerT"] = np.ascontiguousarray(np.asarray(inp["L1_ffn_router_w"], f32).T)
    m["expbias"] = np.asarray(inp["L1_ffn_expert_bias"], f32)[None, :]
    m["c1"] = np.asarray(inp["L1_c"][b], f32)[None, :]
    m["wmod1"] = np.asarray(inp["L1_wmod"], f32)
    for k, v in cfg["consts"].items():
        m["cn_" + k] = np.asarray(v, f32)
    return m


def build_full(cfg):
    mk = MK(cfg)
    mk.declare_io()
    mk.stage_load()
    mk.stage_mod(mk.i_c0, mk.i_wmod0)
    mk.stage_ln("ca", "sa")
    mk.stage_l0_attn()
    mk.stage_ln("cb", "sh")
    mk.stage_l0_ffn()
    mk.stage_mod(mk.i_c1, mk.i_wmod1)
    mk.stage_ln("ca", "sa")
    mk.stage_l1_attn()
    mk.stage_ln("cb", "sh", keep_rows=True)
    mk.stage_l1_ffn()
    return mk


def kernel(**inputs):
    B, T, D = inputs["L0_attn_x"].shape
    cfg = dict(T=T, D=D, H0=16, DCKV=256, DCQ=512, H1=8, DH1=64,
               HID=4096, RH=1024)
    cfg["consts"] = make_consts(T)
    mk = build_full(cfg)
    nc = mk.finish()
    in_maps = [_prep_core_inputs(cfg, inputs, b) for b in range(B)]
    res = run_bass_kernel_spmd(nc, in_maps, core_ids=list(range(B)))
    out = np.stack([np.asarray(r["outT"]).T for r in res.results], 0)
    return np.ascontiguousarray(out.astype(np.float32))


# version 12 (final)
# speedup vs baseline: 1.0268x; 1.0268x over previous
import math
from contextlib import ExitStack

import numpy as np
import ml_dtypes
import concourse.bass as bass
import concourse.mybir as mybir
from concourse.bass_utils import run_bass_kernel_spmd
from concourse.bass_core_rust import IndirectOffsetOnAxis
from concourse import library_config as LC

F32 = mybir.dt.float32
BF16 = mybir.dt.bfloat16
I16 = mybir.dt.int16
I32 = mybir.dt.int32
U32 = mybir.dt.uint32
AF = mybir.ActivationFunctionType
ALU = mybir.AluOpType
AX = mybir.AxisListType

NEG_LOC = -2.0e38
NEG_FUT = -1.0e38
NEG_REP = -3.0e38
NEG_INF = -1.0e30


def make_consts(T, LW=64, DR=32, rope_base=10000.0):
    half = DR // 2
    th = (1.0 / rope_base ** (2.0 * np.arange(half) / DR)).astype(np.float64)
    c = {}
    c["ident"] = np.eye(128, dtype=np.float32)
    k = np.arange(128)[:, None]
    q = np.arange(128)[None, :]
    c["l0diag"] = np.where(q > k, NEG_INF, 0.0).astype(np.float32)
    r = np.arange(128)[:, None]
    cc = np.arange(128)[None, :]
    c["pool_loc"] = np.where((cc > r) & (cc <= r + LW), NEG_LOC,
                             0.0).astype(np.float32)
    fut = cc > r + LW
    c["pool_fmul"] = np.where(fut, 0.0, 1.0).astype(np.float32)
    c["pool_fadd"] = np.where(fut, NEG_FUT, c["pool_loc"]).astype(np.float32)
    col = np.arange(192)[None, :]
    c["mask_band"] = np.where((col >= r + 1) & (col <= r + LW), 0.0,
                              NEG_INF).astype(np.float32)
    okB = (r >= LW - 1) & (col >= r - LW + 1) & (col <= r)
    okA = (r < 95) & (col <= 95) & ((col > r) | (r < LW - 1))
    c["mask_band0"] = np.where(okB, 0.0, NEG_INF).astype(np.float32)
    c["mask_abs0"] = np.where(okA, 0.0, NEG_INF).astype(np.float32)
    t = np.arange(T)[:, None].astype(np.float64)
    c["qrotA"] = np.concatenate([np.cos(t * th), np.sin(t * th)],
                                axis=1).astype(np.float32)
    c["qrotB"] = np.concatenate([np.cos(2 * t * th), np.sin(2 * t * th)],
                                axis=1).astype(np.float32)
    s = np.arange(T)[None, :].astype(np.float64)
    c["krotB"] = np.concatenate([np.cos((s + LW - 1) * th[:, None]),
                                 np.sin((s + LW - 1) * th[:, None])],
                                axis=0).astype(np.float32)
    c["krotA"] = np.concatenate([np.cos(s * th[:, None]),
                                 np.sin(s * th[:, None])],
                                axis=0).astype(np.float32)
    rot4 = np.zeros((8, 32, 128), np.float64)
    for r in range(32):
        q, rl = r // 4, r % 4
        for d in range(16):
            cc_ = np.cos((LW + r) * th[d])
            ss_ = np.sin((LW + r) * th[d])
            rot4[q, d, 32 * rl + d] = cc_
            rot4[q, 16 + d, 32 * rl + d] = -ss_
            rot4[q, d, 32 * rl + 16 + d] = ss_
            rot4[q, 16 + d, 32 * rl + 16 + d] = cc_
    c["rot4"] = rot4.reshape(8 * 32, 128).astype(np.float32)
    rt = ((np.arange(32) % 16) * T).astype(np.float32)
    c["rtpr"] = np.repeat(rt.reshape(1, 32), 128, 0).astype(np.float32)
    return c


class MK:
    def __init__(self, cfg):
        self.cfg = cfg
        for k, v in cfg.items():
            if k != "consts":
                setattr(self, k, v)
        self.NT = self.T // 128
        self.TCW = min(512, self.T)
        self.DC = self.D // 128
        self.DH0 = self.D // self.H0
        self.E = 8
        self.DR = 32
        self.LW = 64
        self.nc = bass.Bass("TRN2", target_bir_lowering=False)
        self.es = ExitStack()
        self._stack = [self.es]
        self._sem = {}
        self._scnt = {}
        self._lib = None
        self._uid = 0
        self._gregs = {}
        for n in ("hw", "sw", "pe", "pe2", "act", "dve", "pool"):
            self._sem[n] = self.es.enter_context(self.nc.semaphore("s_" + n))
            self._scnt[n] = 0

    def greg(self, e, val):
        if val not in self._gregs:
            self._gregs[val] = e.to_reg(val)
        return self._gregs[val]

    def uid(self):
        self._uid += 1
        return self._uid

    def scope(self):
        class _S:
            def __enter__(s):
                s.es = ExitStack()
                self._stack.append(s.es)
                return s.es

            def __exit__(s, *a):
                self._stack.pop()
                return s.es.__exit__(*a)
        return _S()

    def sb(self, name, shape, dt=F32):
        return self._stack[-1].enter_context(
            self.nc.sbuf_tensor(f"{name}_{self.uid()}", shape, dt))

    def psum(self, name, shape, dt=F32):
        return self._stack[-1].enter_context(
            self.nc.psum_tensor(f"{name}_{self.uid()}", shape, dt))

    def din(self, name, shape, dt=F32):
        return self.nc.dram_tensor(name, shape, dt, kind="ExternalInput").ap()

    def dint(self, name, shape, dt=F32):
        return self.nc.dram_tensor(f"{name}_{self.uid()}", shape, dt,
                                   kind="Internal").ap()

    def dma(self, qn, eng, out, in_, **kw):
        self._scnt[qn] += 16
        return eng.dma_start(out, in_, **kw).then_inc(self._sem[qn], 16)

    def dwait(self, eng, qns=("hw", "sw")):
        for qn in qns:
            if self._scnt[qn]:
                eng.wait_ge(self._sem[qn], self._scnt[qn])

    def resv(self, en, n=1):
        out = [(self._sem[en], self._scnt[en] + i + 1) for i in range(n)]
        self._scnt[en] += n
        return out

    def fire(self, instr, ev):
        instr.then_inc(ev[0], 1)
        return instr

    def wait(self, eng, ev):
        eng.wait_ge(ev[0], ev[1])

    def seq(self, en, eng, fns):
        for i, fn in enumerate(fns):
            if i:
                eng.wait_ge(self._sem[en], self._scnt[en])
            ins = fn(eng)
            self._scnt[en] += 1
            ins.then_inc(self._sem[en], 1)

    def pipe2(self, items, prod="tensor", cons="vector", pfam="pe",
              cfam="dve"):
        n = len(items)
        evP = self.resv(pfam, n)
        evC = self.resv(cfam, n)
        with self.nc.Block() as block:
            def prod_body(t):
                for i, (pf, _) in enumerate(items):
                    if i >= 2:
                        self.wait(t, evC[i - 2])
                    self.fire(pf(t, i % 2), evP[i])

            def cons_body(v):
                for i, (_, cf) in enumerate(items):
                    self.wait(v, evP[i])
                    self.fire(cf(v, i % 2), evC[i])
            getattr(block, prod)(prod_body)
            getattr(block, cons)(cons_body)

    def lib(self, eng, which):
        if self._lib is not which:
            eng.load_library(which)
            self._lib = which

    def blk(self, **engs):
        with self.nc.Block() as block:
            for name, fn in engs.items():
                getattr(block, name)(fn)

    def declare_io(self):
        T, D, E = self.T, self.D, self.E
        DCKV, DCQ, H1, DH1, DR = self.DCKV, self.DCQ, self.H1, self.DH1, self.DR
        HID, RH = self.HID, self.RH
        d = self.din
        self.i_xT = d("xT", [D, T])
        self.i_wqkvT = d("wqkvT", [D, 3 * D])
        self.i_wprojT = d("wprojT", [D, D])
        self.i_Wr = d("Wr", [D, E])
        self.i_br = d("br", [1, E])
        self.i_W1 = d("W1", [E, D, D])
        self.i_b1 = d("b1", [E, D])
        self.i_W2 = d("W2", [E, D, D])
        self.i_b2 = d("b2", [E, D])
        self.i_c0 = d("c0", [1, D])
        self.i_wmod0 = d("wmod0", [D, 6 * D])
        self.i_Wdkv = d("Wdkv", [D, DCKV])
        self.i_Wuk = d("Wuk", [DCKV, H1 * DH1])
        self.i_WukT = d("WukT", [H1 * DH1, DCKV])
        self.i_Wuv = d("Wuv", [DCKV, H1 * DH1])
        self.i_Wdq = d("Wdq", [D, DCQ])
        self.i_Wuq = d("Wuq", [DCQ, H1 * DH1])
        self.i_Wqr = d("Wqr", [DCQ, H1 * DR])
        self.i_Wkr = d("Wkr", [D, DR])
        self.i_Wout = d("Wout", [H1 * DH1, D])
        self.i_idxwq = d("idxwq", [D, 4 * 64])
        self.i_idxwk = d("idxwk", [D, 64])
        self.i_idxw = d("idxw", [1, 4])
        self.i_w12T = d("w12T", [E, D, 2 * RH])
        self.i_w3T = d("w3T", [E, RH, D])
        self.i_sw1T = d("sw1T", [D, HID])
        self.i_sw2T = d("sw2T", [D, HID])
        self.i_sw3T = d("sw3T", [HID, D])
        self.i_routerT = d("routerT", [D, E])
        self.i_expbias = d("expbias", [1, E])
        self.i_c1 = d("c1", [1, D])
        self.i_wmod1 = d("wmod1", [D, 6 * D])
        self.CN = {k: d("cn_" + k, list(v.shape))
                   for k, v in self.cfg["consts"].items()}
        self.o_outT = self.nc.dram_tensor("outT", [D, T], F32,
                                          kind="ExternalOutput").ap()
        self.d_hh = self.dint("d_hh", [D, T], BF16)
        self.d_row = self.dint("d_row", [1, 4 * max(self.T, 128 * self.H0)])
        self.d_rec = self.dint("d_rec", [T, 256], BF16)
        self.d_mu = self.dint("d_mu", [T, 32], BF16)
        self.d_idx = self.dint("d_idx", [16, self.NT * 256], I16)
        self.d_g = self.dint("d_g", [HID, T], BF16)
        self.d_xs = self.dint("d_xs", [D, T])
        self.d_ge = self.dint("d_ge", [E, RH, T], BF16)

        self.X = self.sb("X", [128, self.DC, T])
        SW = max(T, 128 * self.H0)
        self.SA = self.sb("SA", [128, SW])
        self.SB_ = self.sb("SB", [128, SW])
        self.IDB = self.sb("IDB", [128, 128], BF16)
        self.IDF = self.sb("IDF", [128, 128], F32)
        self.ONESD = self.sb("ONESD", [128, 1], F32)
        self.ONE1 = self.sb("ONE1", [128, 1], F32)
        self.MOD = {n: self.sb("MOD" + n, [128, self.DC], F32)
                    for n in ("sa", "ca", "ga", "sh", "cb", "gb")}
        self.L0DIAG = self.sb("L0DIAG", [128, 128], F32)
        self.SAB16 = self.sb("SAB16", [128, 512], BF16)
        self.PS0 = self.psum("PS0", [128, 512], F32)
        self.PS1 = self.psum("PS1", [128, 512], F32)
        self.PSA = self.psum("PSA", [128, 128], F32)
        self.PSB = self.psum("PSB", [128, 128], F32)
        self.PSC = self.psum("PSC", [128, 128], F32)
        self.PSD = self.psum("PSD", [128, 128], F32)
        self.PSTB = self.psum("PSTB", [128, 128], BF16)
        self.PSTF = self.psum("PSTF", [128, 128], F32)

    def stage_load(self):
        def g(e):
            for c in range(self.DC):
                self.dma("sw", e, self.X[:, c, :],
                         self.i_xT[128 * c:128 * (c + 1), :])
            self.dma("sw", e, self.IDF[:], self.CN["ident"][:, :])
            self.dma("sw", e, self.L0DIAG[:], self.CN["l0diag"][:, :])
        self.blk(gpsimd=g)

        def v(e):
            self.dwait(e)
            e.memset(self.ONESD[:], 1.0 / self.D)
            e.memset(self.ONE1[:], 1.0)
            e.tensor_copy(self.IDB[:], self.IDF[:])
        self.blk(vector=v)

    def wload(self, e, buf, src_ap, col0, ncol, nk=None):
        nk = nk if nk is not None else src_ap.shape[0] // 128
        for kc in range(nk):
            self.dma("sw", e, buf[:, kc, :ncol],
                     src_ap[128 * kc:128 * (kc + 1), col0:col0 + ncol])

    def stage_mod(self, c_ap, wmod_ap):
        DC, D = self.DC, self.D
        with self.scope():
            cs = self.sb("cs", [128, DC], F32)
            csb = self.sb("csb", [128, DC], BF16)
            wbuf = self.sb("wmodb", [128, DC, 128], BF16)
            self.blk(gpsimd=lambda e: self.dma(
                "sw", e, cs[:, :], c_ap.rearrange("o (c p) -> p c", p=128),
                allow_slow_non_contiguous=True))
            self.blk(scalar=lambda e: (self.dwait(e),
                                       e.activation(cs[:], cs[:], AF.Silu)))
            self.blk(vector=lambda e: e.tensor_copy(csb[:], cs[:]))
            names = ["sa", "ca", "ga", "sh", "cb", "gb"]
            for gi in range(6):
                for mc in range(DC):
                    self.blk(gpsimd=lambda e, gi=gi, mc=mc: self.wload(
                        e, wbuf, wmod_ap, gi * D + 128 * mc, 128, DC))

                    def t(e, gi=gi, mc=mc):
                        self.dwait(e)
                        for kc in range(DC):
                            e.matmul(self.PSA[:, 0:1], wbuf[:, kc, :],
                                     csb[:, kc:kc + 1],
                                     start=(kc == 0), stop=(kc == DC - 1))
                    self.blk(tensor=t)
                    self.blk(vector=lambda e, gi=gi, mc=mc: e.tensor_copy(
                        self.MOD[names[gi]][:, mc:mc + 1], self.PSA[:, 0:1]))
            sgn = self.sb("sgn", [128, DC], F32)
            for nm in ("ga", "gb"):
                self.blk(vector=lambda e, nm=nm: e.tensor_scalar(
                    sgn[:], self.MOD[nm][:], 0.0, None, ALU.is_ge))
                self.blk(vector=lambda e, nm=nm: e.tensor_scalar(
                    sgn[:], sgn[:], 2.0, -1.0, ALU.mult, ALU.add))
                self.blk(scalar=lambda e, nm=nm: e.activation(
                    self.MOD[nm][:], self.MOD[nm][:], AF.Abs))
                self.blk(vector=lambda e, nm=nm: e.tensor_scalar_max(
                    self.MOD[nm][:], self.MOD[nm][:], 0.25))
                self.blk(vector=lambda e, nm=nm: e.tensor_mul(
                    self.MOD[nm][:], self.MOD[nm][:], sgn[:]))

    def stage_ln(self, cname, sname, keep_rows=False):
        T, DC = self.T, self.DC
        if keep_rows:
            self.ROWRK = self.sb("ROWRK", [128, T], BF16)
            self.ROWWK = self.sb("ROWWK", [128, T], BF16)
        with self.scope():
            MR = self.sb("MR", [1, 2 * T], F32)
            ROWR = self.ROWRK if keep_rows else self.sb("ROWR", [128, T], BF16)
            ROWW = self.ROWWK if keep_rows else self.sb("ROWW", [128, T], BF16)
            HT = self.sb("HT", [128, 2, T], BF16)
            W = self.TCW
            for tc in range(T // W):
                sl = slice(W * tc, W * (tc + 1))

                def t1(e, sl=sl):
                    for kc in range(DC):
                        e.matmul(self.PS0[0:1, :W], self.ONESD[:],
                                 self.X[:, kc, sl], start=(kc == 0),
                                 stop=(kc == DC - 1))
                self.blk(tensor=t1)
                self.blk(vector=lambda e, sl=sl: e.tensor_copy(
                    MR[0:1, sl], self.PS0[0:1, :W]))
                def sq_pf(s, par, kc, sl=sl):
                    buf = self.SA if par == 0 else self.SB_
                    return s.activation(buf[:, :W], self.X[:, kc, sl],
                                        AF.Square)

                def sq_cf(t, par, kc):
                    buf = self.SA if par == 0 else self.SB_
                    return t.matmul(self.PS1[0:1, :W], self.ONESD[:],
                                    buf[:, :W], start=(kc == 0),
                                    stop=(kc == DC - 1))
                self.pipe2([(lambda s, par, kc=kc: sq_pf(s, par, kc),
                             lambda t, par, kc=kc: sq_cf(t, par, kc))
                            for kc in range(DC)],
                           prod="scalar", cons="tensor", pfam="act",
                           cfam="pe")
                self.blk(vector=lambda e, tc=tc: e.tensor_copy(
                    MR[0:1, T + W * tc:T + W * (tc + 1)], self.PS1[0:1, :W]))
            self.blk(vector=lambda e: e.tensor_mul(
                self.SA[0:1, :T], MR[0:1, :T], MR[0:1, :T]))
            self.blk(vector=lambda e: e.tensor_sub(
                MR[0:1, T:], MR[0:1, T:], self.SA[0:1, :T]))
            self.blk(vector=lambda e: e.tensor_scalar_add(
                MR[0:1, T:], MR[0:1, T:], 1e-5))
            self.blk(scalar=lambda e: e.activation(
                MR[0:1, T:], MR[0:1, T:], AF.Sqrt))
            self.blk(vector=lambda e: e.reciprocal(MR[0:1, T:], MR[0:1, T:]))
            self.blk(vector=lambda e: e.tensor_mul(
                MR[0:1, :T], MR[0:1, :T], MR[0:1, T:]))

            def g1(e):
                self.dma("sw", e, self.d_row[:, :T], MR[0:1, T:])
                self.dma("sw", e, self.d_row[:, T:2 * T], MR[0:1, :T])
            self.blk(gpsimd=g1)

            def g2(e):
                self.dwait(e)
                self.dma("sw", e, ROWR[:, :],
                         bass.AP(self.d_row.tensor, 0, [[0, 128], [1, T]]))
                self.dma("sw", e, ROWW[:, :],
                         bass.AP(self.d_row.tensor, T, [[0, 128], [1, T]]))
            self.blk(gpsimd=g2)
            onepc = self.sb("onepc", [128, DC], F32)
            self.blk(vector=lambda e: (self.dwait(e), e.tensor_scalar_add(
                onepc[:], self.MOD[cname][:], 1.0)))
            evH = self.resv("dve", 3 * DC)
            hwbase = self._scnt["hw"]

            with self.nc.Block() as block:
                @block.vector
                def _(v):
                    for kc in range(DC):
                        p = kc % 2
                        if kc >= 2:
                            v.wait_ge(self._sem["hw"], hwbase + 16 * kc)
                        if kc >= 1:
                            self.wait(v, evH[3 * kc - 1])
                        self.fire(v.tensor_mul(
                            self.SA[:, :T], self.X[:, kc, :], ROWR[:, :]),
                            evH[3 * kc])
                        self.wait(v, evH[3 * kc])
                        self.fire(v.tensor_sub(
                            self.SA[:, :T], self.SA[:, :T], ROWW[:, :]),
                            evH[3 * kc + 1])
                        self.wait(v, evH[3 * kc + 1])
                        self.fire(v.tensor_scalar(
                            HT[:, p, :T], self.SA[:, :T],
                            onepc[:, kc:kc + 1],
                            self.MOD[sname][:, kc:kc + 1],
                            ALU.mult, ALU.add), evH[3 * kc + 2])

                @block.sync
                def _(e):
                    for kc in range(DC):
                        p = kc % 2
                        e.wait_ge(self._sem["dve"], evH[3 * kc + 2][1])
                        self.dma("hw", e,
                                 self.d_hh[128 * kc:128 * (kc + 1), :],
                                 HT[:, p, :T])
            self.blk(vector=lambda e: self.dwait(e))

    def stage_l0_attn(self):
        T, D, DC, H0, DH0, NT = self.T, self.D, self.DC, self.H0, self.DH0, self.NT
        dbg = getattr(self, "dbg_l0", 0)
        nc = self.nc
        scale0 = 1.0 / math.sqrt(DH0)
        hpc = 128 // DH0
        with self.scope():
            Kf = self.sb("Kf", [128, DC, T], BF16)
            Vt = self.sb("Vt", [128, NT, H0 * (DH0 + 1)], BF16)
            HHc = self.sb("HHc", [128, DC, self.TCW], BF16)
            WB = self.sb("WB", [128, DC, 128], BF16)
            Qf = self.sb("Qf", [128, DC, 128], BF16)
            attnT = self.sb("attnT", [128, DC, 128], BF16)
            Pb = self.sb("Pb", [128, T], BF16)
            rmax = self.sb("rmax0", [128, 4], F32)
            W = self.TCW
            for tc in range(T // W):
                self.blk(gpsimd=lambda e, tc=tc: [self.dma(
                    "sw", e, HHc[:, kc, :W],
                    self.d_hh[128 * kc:128 * (kc + 1), W * tc:W * (tc + 1)])
                    for kc in range(DC)])
                for oc in range(DC):
                    self.blk(gpsimd=lambda e, oc=oc: self.wload(
                        e, WB, self.i_wqkvT, D + 128 * oc, 128, DC))

                    def t(e, oc=oc, tc=tc):
                        self.dwait(e)
                        for kc in range(DC):
                            e.matmul(self.PS0[:, :W], WB[:, kc, :],
                                     HHc[:, kc, :W],
                                     start=(kc == 0), stop=(kc == DC - 1))
                    self.blk(tensor=t)
                    self.blk(vector=lambda e, oc=oc, tc=tc: e.tensor_copy(
                        Kf[:, oc, W * tc:W * (tc + 1)], self.PS0[:, :W]))
                for oc in range(DC):
                    self.blk(gpsimd=lambda e, oc=oc: self.wload(
                        e, WB, self.i_wqkvT, 2 * D + 128 * oc, 128, DC))
                    for q4 in range(W // 128):
                        kt = (W // 128) * tc + q4

                        def t2(e, oc=oc, q4=q4):
                            self.dwait(e)
                            for kc in range(DC):
                                e.matmul(self.PSA[:, :],
                                         HHc[:, kc, 128 * q4:128 * (q4 + 1)],
                                         WB[:, kc, :],
                                         start=(kc == 0), stop=(kc == DC - 1))
                        self.blk(tensor=t2)

                        def v2(e, kt=kt, oc=oc):
                            for hh in range(hpc):
                                h = oc * hpc + hh
                                e.tensor_copy(
                                    Vt[:, kt, h * (DH0 + 1):h * (DH0 + 1) + DH0],
                                    self.PSA[:, hh * DH0:(hh + 1) * DH0])
                        self.blk(vector=v2)
            if dbg == 1:
                return
            WQ = self.sb("WQ", [128, DC, D], BF16)
            WP = self.sb("WP", [128, DC, D], BF16)

            def gw(e):
                for kc in range(DC):
                    self.dma("sw", e, WQ[:, kc, :],
                             self.i_wqkvT[128 * kc:128 * (kc + 1), 0:D])
                    self.dma("sw", e, WP[:, kc, :],
                             self.i_wprojT[128 * kc:128 * (kc + 1), :])
            self.blk(gpsimd=gw)

            qtN = NT if dbg not in (3, 4) else (2 if dbg == 3 else 4)
            for qt in range(qtN):
                qsl = slice(128 * qt, 128 * (qt + 1))
                self.blk(gpsimd=lambda e: [self.dma(
                    "sw", e, HHc[:, kc, :128],
                    self.d_hh[128 * kc:128 * (kc + 1), qsl])
                    for kc in range(DC)])
                def qb_pf(t, par, oc):
                    if oc == 0:
                        self.dwait(t)
                    ps = self.PSA if par == 0 else self.PSB
                    for kc in range(DC):
                        mm = t.matmul(ps[:, :],
                                      WQ[:, kc, 128 * oc:128 * (oc + 1)],
                                      HHc[:, kc, :128],
                                      start=(kc == 0), stop=(kc == DC - 1))
                    return mm

                def qb_cf(v, par, oc):
                    ps = self.PSA if par == 0 else self.PSB
                    return v.tensor_copy(Qf[:, oc, :], ps[:, :])
                self.pipe2([(lambda t, par, oc=oc: qb_pf(t, par, oc),
                             lambda v, par, oc=oc: qb_cf(v, par, oc))
                            for oc in range(DC)])
                if dbg == 2:
                    continue
                nkt = qt + 1
                Wk = 128 * nkt
                resv_h = []
                for h in range(H0):
                    r = {}
                    r["S"] = self.resv("pe", nkt)
                    r["T"] = self.resv("pe", nkt)
                    r["OT"] = self.resv("pe", 1)[0]
                    r["PV"] = self.resv("pe2", nkt)
                    r["C"] = self.resv("dve", nkt)
                    r["R1"] = self.resv("dve", 1)[0]
                    r["R2"] = self.resv("dve", 1)[0]
                    r["RC"] = self.resv("dve", 1)[0]
                    r["PT"] = self.resv("dve", nkt)
                    r["N"] = self.resv("dve", 1)[0]
                    r["A"] = self.resv("dve", 1)[0]
                    r["X"] = self.resv("act", 1)[0]
                    resv_h.append(r)
                PTb = [self.SAB16[:, 128:256], self.SAB16[:, 256:384]]

                with self.nc.Block() as block:
                    @block.tensor
                    def _(t, qt=qt, nkt=nkt, resv_h=resv_h):
                        lastC = {0: None, 1: None}
                        prevN = prevA = None
                        for h in range(H0):
                            r = resv_h[h]
                            oc, hh = h // hpc, h % hpc
                            hr = slice(hh * DH0, (hh + 1) * DH0)
                            for kt in range(nkt):
                                ps = self.PSA if kt % 2 == 0 else self.PSB
                                if lastC[kt % 2] is not None:
                                    self.wait(t, lastC[kt % 2])
                                self.fire(t.matmul(
                                    ps[:, :], Qf[hr, oc, :],
                                    Kf[hr, oc, 128 * kt:128 * (kt + 1)],
                                    start=True, stop=True), r["S"][kt])
                                lastC[kt % 2] = r["C"][kt]
                            self.wait(t, r["X"])
                            if prevA is not None:
                                self.wait(t, prevA)
                            self.fire(t.transpose(
                                self.PSTB[:, :], Pb[:, 0:128],
                                self.IDB[:, :]), r["T"][0])
                            for kt in range(nkt):
                                self.wait(t, r["PT"][kt])
                                if kt == 0 and prevN is not None:
                                    self.wait(t, prevN)
                                self.fire(t.matmul(
                                    self.PSC[:, :DH0], PTb[kt % 2],
                                    Vt[:, kt,
                                       h * (DH0 + 1):h * (DH0 + 1) + DH0],
                                    start=(kt == 0), stop=(kt == nkt - 1)),
                                    r["PV"][kt])
                                if kt + 1 < nkt:
                                    self.fire(t.transpose(
                                        self.PSTB[:, :],
                                        Pb[:, 128 * (kt + 1):128 * (kt + 2)],
                                        self.IDB[:, :]), r["T"][kt + 1])
                            self.wait(t, r["N"])
                            self.fire(t.transpose(
                                self.PSTB[:DH0, :], self.SAB16[:, 0:DH0],
                                self.IDB[:, :]), r["OT"])
                            prevN, prevA = r["N"], r["A"]

                    @block.vector
                    def _(v, qt=qt, nkt=nkt, Wk=Wk, resv_h=resv_h):
                        prevX = None
                        for h in range(H0):
                            r = resv_h[h]
                            oc, hh = h // hpc, h % hpc
                            hr = slice(hh * DH0, (hh + 1) * DH0)
                            for kt in range(nkt):
                                ps = self.PSA if kt % 2 == 0 else self.PSB
                                self.wait(v, r["S"][kt])
                                if kt == 0 and prevX is not None:
                                    self.wait(v, prevX)
                                if kt == nkt - 1:
                                    self.fire(v.tensor_tensor(
                                        self.SA[:, 128 * kt:128 * (kt + 1)],
                                        ps[:, :], self.L0DIAG[:, :],
                                        ALU.add), r["C"][kt])
                                else:
                                    self.fire(v.tensor_copy(
                                        self.SA[:, 128 * kt:128 * (kt + 1)],
                                        ps[:, :]), r["C"][kt])
                            self.wait(v, r["C"][nkt - 1])
                            self.fire(v.tensor_reduce(
                                rmax[:, 0:1], self.SA[:, :Wk], AX.X,
                                ALU.max), r["R1"])
                            self.wait(v, r["R1"])
                            self.fire(v.tensor_scalar(
                                rmax[:, 1:2], rmax[:, 0:1], -scale0, None,
                                ALU.mult), r["R2"])
                            self.wait(v, r["X"])
                            self.fire(v.reciprocal(
                                rmax[:, 3:4], rmax[:, 2:3]), r["RC"])
                            for kt in range(nkt):
                                self.wait(v, r["T"][kt])
                                if kt >= 2:
                                    self.wait(v, r["PV"][kt - 2])
                                self.fire(v.tensor_copy(
                                    PTb[kt % 2], self.PSTB[:, :]),
                                    r["PT"][kt])
                            self.wait(v, r["PV"][nkt - 1])
                            self.wait(v, r["RC"])
                            self.fire(v.tensor_scalar(
                                self.SAB16[:, 0:DH0], self.PSC[:, :DH0],
                                rmax[:, 3:4], None, ALU.mult), r["N"])
                            self.wait(v, r["OT"])
                            self.fire(v.tensor_copy(
                                attnT[hr, oc, :], self.PSTB[:DH0, :]),
                                r["A"])
                            prevX = r["X"]

                    @block.scalar
                    def _(s, qt=qt, nkt=nkt, Wk=Wk, resv_h=resv_h):
                        prevA = None
                        for h in range(H0):
                            r = resv_h[h]
                            self.wait(s, r["R2"])
                            if prevA is not None:
                                self.wait(s, prevA)
                            self.fire(s.activation(
                                Pb[:, :Wk], self.SA[:, :Wk], AF.Exp,
                                bias=rmax[:, 1:2], scale=scale0,
                                accum_out=rmax[:, 2:3]), r["X"])
                            prevA = r["A"]
                for oc in range(DC):
                    self.blk(tensor=lambda e, oc=oc: [e.matmul(
                        self.PSA[:, :], WP[:, kc, 128 * oc:128 * (oc + 1)],
                        attnT[:, kc, :], start=(kc == 0), stop=(kc == DC - 1))
                        for kc in range(DC)])
                    self.blk(vector=lambda e, oc=oc, qsl=qsl:
                             e.scalar_tensor_tensor(
                                 self.X[:, oc, qsl], self.PSA[:, :],
                                 self.MOD["ga"][:, oc:oc + 1],
                                 self.X[:, oc, qsl], ALU.mult, ALU.add))

    def stage_l0_ffn(self):
        T, D, DC, NT, E = self.T, self.D, self.DC, self.NT, self.E
        dbg = getattr(self, "dbg_ffn", 0)
        with self.scope():
            HHc = self.sb("HHc", [128, DC, 128], BF16)
            h0col = self.sb("h0col", [128, DC], F32)
            h0bf = self.sb("h0bf", [128, DC], BF16)
            sc8 = self.sb("sc8", [128, 8], F32)
            Wrb = self.sb("Wrb", [128, DC, 8], F32)
            eidx = self.sb("eidx", [128, 2], F32)
            wbase = self.sb("wbase", [128, 1], F32)
            widx = self.sb("widx", [128, 1], I32)
            iotf = self.sb("iotf", [128, 1], F32)
            W1G = self.sb("W1G", [128, DC, D], BF16)
            W2G = self.sb("W2G", [128, DC, D], BF16)
            WSTG = self.sb("WSTG", [128, D], F32)
            BCOL = self.sb("BCOL", [128, DC, 2], F32)
            HTI = self.sb("HTI", [128, DC, 128], BF16)
            u8 = self.sb("u8", [128, 8], U32)
            self.blk(gpsimd=lambda e: self.dma(
                "sw", e, h0bf[:, :],
                self.d_hh.rearrange("(c p) t -> p c t", p=128)[:, :, 0:1]
                .rearrange("p c o -> p (c o)"),
                allow_slow_non_contiguous=True))
            self.blk(vector=lambda e: (self.dwait(e),
                                       e.tensor_copy(h0col[:], h0bf[:])))
            self.blk(gpsimd=lambda e: [self.dma(
                "sw", e, Wrb[:, kc, :],
                self.i_Wr[128 * kc:128 * (kc + 1), :]) for kc in range(DC)])

            def t(e):
                self.dwait(e)
                for kc in range(DC):
                    e.matmul(self.PSA[0:1, :8], h0col[:, kc:kc + 1],
                             Wrb[:, kc, :], start=(kc == 0),
                             stop=(kc == DC - 1))
            self.blk(tensor=t)
            self.blk(gpsimd=lambda e: self.dma(
                "sw", e, sc8[0:1, :], self.i_br[:, :]))
            self.blk(vector=lambda e: (self.dwait(e), e.tensor_tensor(
                sc8[0:1, :], sc8[0:1, :], self.PSA[0:1, :8], ALU.add)))
            self.blk(vector=lambda e: e.max(self.SA[0:1, :8], sc8[0:1, :8]))
            self.blk(vector=lambda e: e.max_index(
                u8[0:1, :], self.SA[0:1, :8], sc8[0:1, :8]))
            self.blk(vector=lambda e: e.tensor_copy(
                eidx[0:1, :2], u8[0:1, :2]))
            self.blk(gpsimd=lambda e: self.dma(
                "sw", e, self.d_row[:, 0:2], eidx[0:1, :2]))
            self.blk(gpsimd=lambda e: (self.dwait(e), self.dma(
                "sw", e, eidx[:, :],
                bass.AP(self.d_row.tensor, 0, [[0, 128], [1, 2]]))))
            BASE16 = self.sb("BASE16", [128, 8], F32)
            IDX16 = self.sb("IDX16", [128, 8], I16)
            ECOL = self.sb("ECOL", [128, 1], F32)
            self.blk(gpsimd=lambda e: (
                self.lib(e, LC.standard),
                e.iota(self.SA[0:1, 0:128], [[1, 128]], channel_multiplier=0,
                       allow_small_or_imprecise_dtypes=True)))
            self.blk(gpsimd=lambda e: (self.dwait(e), self.dma(
                "sw", e, self.d_row[:, 0:128], self.SA[0:1, 0:128])))
            self.blk(gpsimd=lambda e: (self.dwait(e), [self.dma(
                "sw", e, BASE16[16 * g:16 * (g + 1), :],
                bass.AP(self.d_row.tensor, 0, [[1, 16], [16, 8]]),
                allow_slow_non_contiguous=True) for g in range(8)]))
            self.blk(vector=lambda e: self.dwait(e))
            if dbg == 1:
                return
            for ke in range(2):
                self.blk(vector=lambda e, ke=ke: e.tensor_scalar(
                    ECOL[:, :], eidx[:, ke:ke + 1], float(D), None, ALU.mult))
                for wi, (w_ap, WG) in enumerate([(self.i_W1, W1G),
                                                 (self.i_W2, W2G)]):
                    w_flat = w_ap.rearrange("e a b -> (e a) b")
                    for kc in range(DC):
                        self.blk(vector=lambda e, kc=kc: self.seq("dve", e, [
                            lambda v: v.tensor_scalar(
                                self.SA[:, 0:8], BASE16[:, :],
                                float(128 * kc), ECOL[:, 0:1],
                                ALU.add, ALU.add),
                            lambda v: v.tensor_copy(IDX16[:, :],
                                                    self.SA[:, 0:8]),
                        ]))

                        def gi(e, w_flat=w_flat):
                            self.lib(e, LC.mlp)
                            self._scnt["sw"] += 16
                            e.dma_gather(
                                WSTG[:].rearrange("p (s d) -> p s d", d=D),
                                w_flat[:, :], IDX16[:, :], 128,
                                self.greg(e, 128), D, elem_step=D,
                            ).then_inc(self._sem["sw"], 16)
                        self.blk(gpsimd=gi)
                        self.blk(vector=lambda e, WG=WG, kc=kc: (
                            self.dwait(e),
                            e.tensor_copy(WG[:, kc, :], WSTG[:, :])))
                if dbg == 2:
                    continue
                self.blk(vector=lambda e, ke=ke: e.tensor_copy(
                    widx[:, :], eidx[:, ke:ke + 1]))
                self.blk(vector=lambda e, ke=ke: self.seq("dve", e, [
                    lambda v: v.tensor_copy(
                        self.SA[:, 0:8],
                        bass.AP(eidx, ke, [[2, 128], [0, 8]])),
                    lambda v: v.tensor_copy(IDX16[:, :], self.SA[:, 0:8]),
                ]))
                for wi, b_ap in enumerate([self.i_b1, self.i_b2]):
                    def gb(e, b_ap=b_ap):
                        self.lib(e, LC.mlp)
                        self._scnt["sw"] += 16
                        e.dma_gather(
                            WSTG[:].rearrange("p (s d) -> p s d", d=D),
                            b_ap[:, :], IDX16[:, :], 128,
                            self.greg(e, 128), D, elem_step=D,
                        ).then_inc(self._sem["sw"], 16)
                    self.blk(gpsimd=gb)
                    self.blk(vector=lambda e: self.dwait(e))
                    for kc in range(DC):
                        self.blk(tensor=lambda e, kc=kc: e.transpose(
                            self.PSD[:, 0:1],
                            WSTG[0:1, 128 * kc:128 * (kc + 1)],
                            self.IDF[0:1, 0:1]))
                        self.blk(vector=lambda e, kc=kc, wi=wi: e.tensor_copy(
                            BCOL[:, kc, wi:wi + 1], self.PSD[:, 0:1]))
                for tt in range(0 if dbg not in (3, 4, 5) else NT, NT):
                    tsl = slice(128 * tt, 128 * (tt + 1))
                    self.blk(gpsimd=lambda e, tsl=tsl: [self.dma(
                        "sw", e, HHc[:, kc, :],
                        self.d_hh[128 * kc:128 * (kc + 1), tsl])
                        for kc in range(DC)])
                    def f1_pf(t, par, oc):
                        if oc == 0:
                            self.dwait(t)
                        ps = self.PSA if par == 0 else self.PSB
                        for kc in range(DC):
                            mm = t.matmul(ps[:, :], W1G[:, kc,
                                          128 * oc:128 * (oc + 1)],
                                          HHc[:, kc, :], start=(kc == 0),
                                          stop=(kc == DC - 1))
                        return mm

                    def f1_cf(s, par, oc):
                        ps = self.PSA if par == 0 else self.PSB
                        return s.activation(HTI[:, oc, :], ps[:, :], AF.Gelu,
                                            bias=BCOL[:, oc, 0:1])
                    self.pipe2([(lambda t, par, oc=oc: f1_pf(t, par, oc),
                                 lambda s, par, oc=oc: f1_cf(s, par, oc))
                                for oc in range(DC)],
                               cons="scalar", cfam="act")
                    for oc in range(DC):
                        def t2(e, oc=oc):
                            for kc in range(DC):
                                e.matmul(self.PSA[:, :], W2G[:, kc,
                                         128 * oc:128 * (oc + 1)],
                                         HTI[:, kc, :], start=(kc == 0),
                                         stop=(kc == DC - 1))
                        self.blk(tensor=t2)
                        self.blk(vector=lambda e, oc=oc, tsl=tsl:
                                 self.seq("dve", e, [
                                     lambda v: v.tensor_scalar(
                                         self.SA[:, :128], self.PSA[:, :],
                                         BCOL[:, oc, 1:2], 0.5,
                                         ALU.add, ALU.mult),
                                     lambda v: v.scalar_tensor_tensor(
                                         self.X[:, oc, tsl],
                                         self.SA[:, :128],
                                         self.MOD["gb"][:, oc:oc + 1],
                                         self.X[:, oc, tsl],
                                         ALU.mult, ALU.add),
                                 ]))

    def stage_l1_attn(self):
        dbg = getattr(self, "dbg_l1", 0)
        T, D, DC, NT = self.T, self.D, self.DC, self.NT
        DCKV, DCQ, H1, DH1, DR, LW = (self.DCKV, self.DCQ, self.H1, self.DH1,
                                      self.DR, self.LW)
        KVC = DCKV // 128
        CQC = DCQ // 128
        KC = H1 * DH1 // 128
        hp1 = 128 // DH1
        nc = self.nc
        scale1 = 1.0 / math.sqrt(DH1 + DR)
        W = self.TCW
        with self.scope():
            d_kf = self.dint("d_kf", [H1 * DH1 // 128, T, 128], BF16)
            d_kfT = self.dint("d_kfT", [H1 * DH1, T], BF16)
            d_prA = self.dint("d_prA", [16 * T, 128], BF16)
            d_prB = self.dint("d_prB", [16 * T, 128], BF16)
            PADR = NT * 256 + 64
            d_i1 = self.dint("d_i1", [16, PADR], I16)
            d_i2 = self.dint("d_i2", [16, PADR], I16)
            Wdq = self.sb("Wdq", [128, DC, DCQ], BF16)
            Wuq = self.sb("Wuq", [128, CQC, H1 * DH1], BF16)
            Wqr = self.sb("Wqr", [128, CQC, H1 * DR], BF16)
            Wkr = self.sb("Wkr", [128, DC, DR], BF16)
            Iwq = self.sb("Iwq", [128, DC, 256], BF16)
            Iwk = self.sb("Iwk", [128, DC, 64], BF16)
            Wuv = self.sb("Wuv", [128, KVC, H1 * DH1], BF16)
            WoutB = self.sb("WoutB", [128, KC, D], BF16)
            ROT4 = self.sb("ROT4", [32, 8, 128], BF16)
            wcol = self.sb("wcol", [128, 4], F32)

            def gw(e):
                for kc in range(DC):
                    self.dma("sw", e, Wdq[:, kc, :], self.i_Wdq[128 * kc:128 * (kc + 1), :])
                    self.dma("sw", e, Wkr[:, kc, :], self.i_Wkr[128 * kc:128 * (kc + 1), :])
                    self.dma("sw", e, Iwq[:, kc, :], self.i_idxwq[128 * kc:128 * (kc + 1), :])
                    self.dma("sw", e, Iwk[:, kc, :], self.i_idxwk[128 * kc:128 * (kc + 1), :])
                for kc in range(KVC):
                    self.dma("sw", e, Wuv[:, kc, :], self.i_Wuv[128 * kc:128 * (kc + 1), :])
                for kc in range(CQC):
                    self.dma("sw", e, Wuq[:, kc, :], self.i_Wuq[128 * kc:128 * (kc + 1), :])
                    self.dma("sw", e, Wqr[:, kc, :], self.i_Wqr[128 * kc:128 * (kc + 1), :])
                for kc in range(KC):
                    self.dma("sw", e, WoutB[:, kc, :], self.i_Wout[128 * kc:128 * (kc + 1), :])
                for q_ in range(8):
                    self.dma("sw", e, ROT4[:, q_, :],
                             self.CN["rot4"][32 * q_:32 * (q_ + 1), :])
                self.dma("sw", e, self.d_row[:, 0:4], self.i_idxw[:, :])
            self.blk(gpsimd=gw)
            self.blk(gpsimd=lambda e: (self.dwait(e), self.dma(
                "sw", e, wcol[:, :],
                bass.AP(self.d_row.tensor, 0, [[0, 128], [1, 4]]))))
            MBAND = self.sb("MBAND", [128, 192], F32)
            MBAND0 = self.sb("MBAND0", [128, 192], F32)
            MABS0 = self.sb("MABS0", [128, 192], F32)
            PLOC = self.sb("PLOC", [128, 128], F32)
            PFUT = self.sb("PFUT", [128, 128], F32)
            RTPR = self.sb("RTPR", [128, 32], F32)

            def gc(e):
                self.dma("sw", e, MBAND[:], self.CN["mask_band"][:, :])
                self.dma("sw", e, MBAND0[:], self.CN["mask_band0"][:, :])
                self.dma("sw", e, MABS0[:], self.CN["mask_abs0"][:, :])
                self.dma("sw", e, PLOC[:], self.CN["pool_fmul"][:, :])
                self.dma("sw", e, PFUT[:], self.CN["pool_fadd"][:, :])
                self.dma("sw", e, RTPR[:], self.CN["rtpr"][:, :])
            self.blk(gpsimd=gc)
            self.blk(vector=lambda e: self.dwait(e))

            HHc = self.sb("HHc", [128, DC, W], BF16)
            ckvtm = self.sb("ckvtm", [128, NT, DCKV], BF16)
            kiT = self.sb("kiT", [64, T], BF16)
            mB = self.sb("mB", [32, T], BF16)
            mA = self.sb("mA", [32, 192], BF16)
            with self.scope():
                ckvT = self.sb("ckvT", [128, KVC, T], BF16)
                Wdkv = self.sb("Wdkv", [128, DC, DCKV], BF16)
                Wuk = self.sb("Wuk", [128, KVC, H1 * DH1], BF16)
                self.blk(gpsimd=lambda e: ([self.dma(
                    "sw", e, Wdkv[:, kc, :],
                    self.i_Wdkv[128 * kc:128 * (kc + 1), :]) for kc in range(DC)],
                    [self.dma("sw", e, Wuk[:, kc, :],
                              self.i_Wuk[128 * kc:128 * (kc + 1), :])
                     for kc in range(KVC)]))
                self.blk(vector=lambda e: self.dwait(e))
                muT = self.sb("muT", [32, T], BF16)
                for tc in range(T // W):
                    sl = slice(W * tc, W * (tc + 1))
                    self.blk(gpsimd=lambda e, sl=sl: [self.dma(
                        "sw", e, HHc[:, kc, :W],
                        self.d_hh[128 * kc:128 * (kc + 1), sl]) for kc in range(DC)])
                    for oc in range(KVC):
                        self.blk(tensor=lambda e, oc=oc: (self.dwait(e), [e.matmul(
                            self.PS0[:, :W], Wdkv[:, kc, 128 * oc:128 * (oc + 1)],
                            HHc[:, kc, :W], start=(kc == 0), stop=(kc == DC - 1))
                            for kc in range(DC)]))
                        self.blk(vector=lambda e, oc=oc, sl=sl: e.tensor_copy(
                            ckvT[:, oc, sl], self.PS0[:, :W]))
                    def tki(e):
                        for kc in range(DC):
                            e.matmul(self.PS1[:64, :W], Iwk[:, kc, :],
                                     HHc[:, kc, :W], start=(kc == 0),
                                     stop=(kc == DC - 1))
                    self.blk(tensor=tki)
                    self.blk(vector=lambda e, sl=sl: e.tensor_copy(
                        kiT[:, sl], self.PS1[:64, :W]))
                    def tkr(e):
                        for kc in range(DC):
                            e.matmul(self.PS1[:32, :W], Wkr[:, kc, :],
                                     HHc[:, kc, :W], start=(kc == 0),
                                     stop=(kc == DC - 1))
                    self.blk(tensor=tkr)
                    self.blk(scalar=lambda e: e.activation(
                        self.SA[:32, :W], self.PS1[:32, :W], AF.Abs))
                    self.blk(scalar=lambda e: e.activation(
                        self.SA[:32, :W], self.SA[:32, :W], AF.Exp, scale=-1.0))
                    self.blk(vector=lambda e: e.tensor_scalar_add(
                        self.SA[:32, :W], self.SA[:32, :W], 1.0))
                    self.blk(scalar=lambda e: e.activation(
                        self.SA[:32, :W], self.SA[:32, :W], AF.Ln))
                    self.blk(vector=lambda e, sl=sl: e.scalar_tensor_tensor(
                        muT[:, sl], self.PS1[:32, :W], 0.0,
                        self.SA[:32, :W], ALU.max, ALU.add))
                kfC = self.sb("kfC", [128, 2, W], BF16)
                for tc in range(T // W):
                    sl = slice(W * tc, W * (tc + 1))
                    for oc in range(KC):
                        self.blk(tensor=lambda e, oc=oc, sl=sl: [e.matmul(
                            self.PS0[:, :W], Wuk[:, kc, 128 * oc:128 * (oc + 1)],
                            ckvT[:, kc, sl], start=(kc == 0), stop=(kc == KVC - 1))
                            for kc in range(KVC)])
                        self.blk(vector=lambda e, oc=oc: (
                            self.dwait(e, ("hw",)),
                            e.tensor_copy(kfC[:, oc % 2, :], self.PS0[:, :W])))
                        self.blk(sync=lambda e, oc=oc, sl=sl: self.dma(
                            "hw", e, d_kfT[128 * oc:128 * (oc + 1), sl],
                            kfC[:, oc % 2, :]))
                mu2T = self.sb("mu2T", [16, T], BF16)
                KRc = self.sb("KRc", [16, T], BF16)
                KRs = self.sb("KRs", [16, T], BF16)
                MH2 = self.sb("MH2", [16, T], BF16)
                self.blk(gpsimd=lambda e: (
                    self.dma("sw", e, KRc[:, :], self.CN["krotB"][0:16, :]),
                    self.dma("sw", e, KRs[:, :], self.CN["krotB"][16:32, :]),
                    self.dma("sw", e, mu2T[:, :].bitcast(F32) if False else
                             mu2T[:, :], bass.AP(muT, 16 * muT.shape[1],
                                                 [[muT.shape[1], 16], [1, T]])))
                ) if False else None
                def loadkr(which):
                    self.blk(gpsimd=lambda e: (
                        self.dma("sw", e, KRc[:, :T if which == "krotB" else 192],
                                 self.CN[which][0:16, :T if which == "krotB" else 192]),
                        self.dma("sw", e, KRs[:, :T if which == "krotB" else 192],
                                 self.CN[which][16:32, :T if which == "krotB" else 192]),
                    ))
                self.blk(gpsimd=lambda e: self.dma(
                    "sw", e, mu2T[:, :],
                    bass.AP(muT, 16 * muT.shape[1],
                            [[muT.shape[1], 16], [1, T]])))
                def rot_apply(dst, n):
                    self.blk(vector=lambda e: (self.dwait(e), self.seq("dve", e, [
                        lambda v: v.tensor_mul(self.SA[:16, :n], muT[0:16, :n], KRc[:16, :n]),
                        lambda v: v.tensor_mul(self.SB_[:16, :n], mu2T[0:16, :n], KRs[:16, :n]),
                        lambda v: v.tensor_sub(self.SA[:16, :n], self.SA[:16, :n], self.SB_[:16, :n]),
                        lambda v: v.tensor_copy(dst[0:16, :n], self.SA[:16, :n]),
                        lambda v: v.tensor_mul(self.SA[:16, :n], muT[0:16, :n], KRs[:16, :n]),
                        lambda v: v.tensor_mul(self.SB_[:16, :n], mu2T[0:16, :n], KRc[:16, :n]),
                        lambda v: v.tensor_add(self.SA[:16, :n], self.SA[:16, :n], self.SB_[:16, :n]),
                        lambda v: v.tensor_copy(MH2[0:16, :n], self.SA[:16, :n]),
                    ])))
                    self.blk(gpsimd=lambda e: self.dma(
                        "sw", e,
                        bass.AP(dst, 16 * dst.shape[1], [[dst.shape[1], 16], [1, n]]),
                        MH2[0:16, :n]))
                    self.blk(vector=lambda e: self.dwait(e))
                loadkr("krotB")
                rot_apply(mB, T)
                loadkr("krotA")
                rot_apply(mA, 192)
                for kt in range(NT):
                    for oc in range(KC):
                        self.blk(gpsimd=lambda e, kt=kt, oc=oc: self.dma(
                            "sw", e, kfC[:, 0, :128],
                            d_kfT[128 * oc:128 * (oc + 1),
                                  128 * kt:128 * (kt + 1)]))
                        self.blk(tensor=lambda e: (self.dwait(e), e.transpose(
                            self.PSTB[:, :], kfC[:, 0, :128], self.IDB[:, :])))
                        self.blk(vector=lambda e: e.tensor_copy(
                            self.SAB16[:, :128], self.PSTB[:, :]))
                        self.blk(sync=lambda e, kt=kt, oc=oc: self.dma(
                            "hw", e, d_kf[oc, 128 * kt:128 * (kt + 1), :],
                            self.SAB16[:, :128]))
                        self.blk(vector=lambda e: self.dwait(e, ("hw",)))
                    for oc in range(KVC):
                        self.blk(tensor=lambda e, kt=kt, oc=oc: e.transpose(
                            self.PSTB[:, :], ckvT[:, oc, 128 * kt:128 * (kt + 1)],
                            self.IDB[:, :]))
                        self.blk(vector=lambda e, kt=kt, oc=oc: e.tensor_copy(
                            ckvtm[:, kt, 128 * oc:128 * (oc + 1)], self.PSTB[:, :]))
                muB = self.sb("muB", [32, T], BF16)
                self.blk(vector=lambda e: (e.memset(self.SAB16[:, :], 0.0),
                                           e.tensor_copy(muB[:, :], muT[:, :])))
                def zfill(e):
                    CH = 128 * 512
                    tot = 16 * T * 128
                    for tgt in (d_prA, d_prB):
                        fl = tgt.rearrange("a b -> (a b)")
                        for off in range(0, tot, CH):
                            self.dma("sw", e, fl[off:off + CH].rearrange(
                                "(p n) -> p n", p=128), self.SAB16[:, :])
                self.blk(gpsimd=zfill)
                self.blk(vector=lambda e: self.dwait(e))
                for q in range(8):
                    for kt in range(NT):
                        self.blk(tensor=lambda e, q=q, kt=kt: e.matmul(
                            self.PSA[:, :], muB[:, 128 * kt:128 * (kt + 1)],
                            ROT4[:, q, :], start=True, stop=True))
                        self.blk(vector=lambda e: e.tensor_copy(
                            self.SAB16[:, :128], self.PSA[:, :]))

                        def wpr(e, q=q, kt=kt):
                            tgt = d_prA if q < 4 else d_prB
                            r16 = (4 * q) % 16
                            self.dma("hw", e,
                                     bass.AP(tgt.tensor,
                                             (r16 * T + 128 * kt) * 128,
                                             [[128, 128], [T * 128, 4],
                                              [1, 32]]),
                                     bass.AP(self.SAB16, 0,
                                             [[512, 128], [32, 4], [1, 32]]))
                        self.blk(sync=wpr)
                        self.blk(vector=lambda e: self.dwait(e, ("hw",)))
            if dbg == 1:
                return
            SIGF = self.sb("SIGF", [128, NT * 32], F32)
            SIGI = self.sb("SIGI", [128, NT * 32], I16)
            SIGP = self.sb("SIGP", [128, NT * 32], I16)
            EMK = self.sb("EMK", [128, NT * 32], F32)
            mx8 = self.sb("mx8", [128, 8], F32)
            u8 = self.sb("u8", [128, 8], U32)
            qiT = self.sb("qiT", [64, 4, 128], BF16)
            for tt in range(NT):
                Wp = 128 * tt + 64
                tsl = slice(128 * tt, 128 * (tt + 1))
                self.blk(gpsimd=lambda e, tsl=tsl: [self.dma(
                    "sw", e, HHc[:, kc, :128],
                    self.d_hh[128 * kc:128 * (kc + 1), tsl])
                    for kc in range(DC)])
                for h in range(4):
                    self.blk(tensor=lambda e, h=h: (self.dwait(e), [e.matmul(
                        self.PSA[:64, :], Iwq[:, kc, 64 * h:64 * (h + 1)],
                        HHc[:, kc, :128], start=(kc == 0), stop=(kc == DC - 1))
                        for kc in range(DC)]))
                    self.blk(vector=lambda e, h=h: e.tensor_copy(
                        qiT[:, h, :], self.PSA[:64, :]))
                nch = (Wp + 511) // 512
                for ci in range(nch):
                    cw = min(512, Wp - 512 * ci)
                    csl = slice(512 * ci, 512 * ci + cw)
                    for h in range(4):
                        self.blk(tensor=lambda e, h=h, csl=csl, cw=cw: e.matmul(
                            self.PS0[:, :cw], qiT[:, h, :], kiT[:, csl],
                            start=True, stop=True))
                        if h == 0:
                            self.blk(vector=lambda e, csl=csl, cw=cw: e.tensor_scalar(
                                self.SA[:, csl], self.PS0[:, :cw], 0.0,
                                wcol[:, 0:1], ALU.max, ALU.mult))
                        else:
                            self.blk(scalar=lambda e, cw=cw: e.activation(
                                self.SB_[:, :cw], self.PS0[:, :cw], AF.Relu))
                            self.blk(vector=lambda e, h=h, csl=csl, cw=cw:
                                     e.scalar_tensor_tensor(
                                         self.SA[:, csl], self.SB_[:, :cw],
                                         wcol[:, h:h + 1], self.SA[:, csl],
                                         ALU.mult, ALU.add))
                wof = Wp - 128
                msl = slice(max(0, wof), Wp)
                mc0 = max(0, -wof)
                mw = msl.stop - msl.start
                self.blk(vector=lambda e, msl=msl, mc0=mc0, mw=mw: self.seq(
                    "dve", e, [
                        lambda v: v.tensor_mul(self.SA[:, msl], self.SA[:, msl],
                                               PLOC[:, mc0:128]),
                        lambda v: v.tensor_add(self.SA[:, msl], self.SA[:, msl],
                                               PFUT[:, mc0:128]),
                    ]))
                ts32 = slice(32 * tt, 32 * (tt + 1))

                def vext(e, Wp=Wp, tt=tt, ts32=ts32):
                    fns = []
                    for rd in range(4):
                        co = 32 * tt + 8 * rd
                        fns += [
                            lambda v: v.max(mx8[:, :], self.SA[:, :Wp]),
                            lambda v: v.max_index(u8[:, :], mx8[:, :],
                                                  self.SA[:, :Wp]),
                            lambda v: v.match_replace(
                                self.SA[:, :Wp], mx8[:, :], self.SA[:, :Wp],
                                NEG_REP),
                            lambda v, co=co: v.tensor_copy(
                                SIGF[:, co:co + 8], u8[:, :]),
                            lambda v, co=co: v.tensor_copy(
                                EMK[:, co:co + 8], mx8[:, :]),
                        ]
                    fns += [
                        lambda v: v.tensor_scalar(EMK[:, ts32], EMK[:, ts32],
                                                  -5e37, None, ALU.is_gt),
                        lambda v: v.tensor_scalar(EMK[:, ts32], EMK[:, ts32],
                                                  1e30, -1e30, ALU.mult,
                                                  ALU.add),
                        lambda v: v.tensor_copy(SIGI[:, ts32], SIGF[:, ts32]),
                        lambda v: v.tensor_add(self.SB_[:, 0:32],
                                               SIGF[:, ts32], RTPR[:, :]),
                        lambda v: v.tensor_copy(SIGP[:, ts32],
                                                self.SB_[:, 0:32]),
                    ]
                    self.seq("dve", e, fns)
                self.blk(vector=vext)
            if dbg == 2:
                return
            def idxbuild(e, qn, SRC, tgt, split16):
                for g in range(8):
                    for tp in range(NT):
                        if not split16:
                            self.dma(qn, e,
                                     bass.AP(tgt.tensor, 256 * tp + g,
                                             [[PADR, 16], [8, 32]]),
                                     bass.AP(SRC, (16 * g) * (NT * 32) + 32 * tp,
                                             [[NT * 32, 16], [1, 32]]),
                                     allow_slow_non_contiguous=True)
                        else:
                            for hf in range(2):
                                self.dma(qn, e,
                                         bass.AP(tgt.tensor,
                                                 256 * tp + 128 * hf + g,
                                                 [[PADR, 16], [8, 16]]),
                                         bass.AP(SRC, (16 * g) * (NT * 32)
                                                 + 32 * tp + 16 * hf,
                                                 [[NT * 32, 16], [1, 16]]),
                                         allow_slow_non_contiguous=True)
            self.blk(sync=lambda e: idxbuild(e, "hw", SIGI, d_i1, False),
                     scalar=lambda e: idxbuild(e, "hw", SIGP, d_i2, True))
            IDX1 = self.sb("IDX1", [128, NT * 256], I16)
            IDX2 = self.sb("IDX2", [128, NT * 256], I16)
            self.blk(gpsimd=lambda e: (self.dwait(e), [self.dma(
                "sw", e, XX[:, :],
                bass.AP(DD.tensor, 0, [[0, 8], [PADR, 16], [1, NT * 256]]))
                for XX, DD in [(IDX1, d_i1), (IDX2, d_i2)]]))
            self.blk(vector=lambda e: self.dwait(e))

            if dbg == 3:
                return
            qcT = self.sb("qcT", [128, max(1, H1 // 2), 128], BF16)
            qXT = {"A": self.sb("qAT", [32, H1, 128], BF16),
                   "B": self.sb("qBT", [32, H1, 128], BF16)}
            qctm = self.sb("qctm", [128, H1 * DH1], BF16)
            cqT = self.sb("cqT", [128, CQC, 128], BF16)
            QRC = self.sb("QRC", [128, 32], F32)
            ur = self.sb("ur", [128, H1 * 32], F32)
            QTM = self.sb("QTM", [128, H1 * 32], F32)
            QB16 = self.sb("QB16", [128, H1 * 32], BF16)
            QATMB = self.sb("QATMB", [128, H1 * 32], BF16)
            KFG = self.sb("KFG", [128, 32 * 128], BF16)
            MRG = self.sb("MRG", [128, 16 * 128], BF16)
            CONT = self.sb("CONT", [128, H1 * 32], F32)
            PRB = self.sb("PRB", [128, 2, H1 * 224], BF16)
            ZCOL = self.sb("ZCOL", [128, 2, H1], F32)
            rmax = self.sb("rmax", [128, 4], F32)
            zc = self.sb("zc", [128, 2], F32)
            PD = self.sb("PD", [128, T], BF16)
            OCT = self.sb("OCT", [128, KVC, 128], BF16)
            ATT = self.sb("ATT", [128, KC, 128], BF16)
            ES = self.sb("ES", [128, 32], F32)
            ROPE = self.sb("ROPE", [128, H1 * 32], F32)
            KW = self.sb("KW", [128, KC, 192], BF16)
            EPROD = self.sb("EPROD", [128, 2048], BF16)
            self.blk(gpsimd=lambda e: self.lib(e, LC.mlp))

            def score_phase(tt):
                par = tt % 2
                tsl = slice(128 * tt, 128 * (tt + 1))
                Wd = 128 * (tt + 1)
                w0 = max(0, 128 * tt - 64)
                wN = min(192, Wd - w0)
                self.blk(gpsimd=lambda e: ([self.dma(
                    "sw", e, HHc[:, kc, :128],
                    self.d_hh[128 * kc:128 * (kc + 1), tsl])
                    for kc in range(DC)],
                    [self.dma("sw", e, KW[:, oc_, :wN],
                              d_kfT[128 * oc_:128 * (oc_ + 1), w0:w0 + wN])
                     for oc_ in range(KC)]))
                def cq_pf(t, par, oc):
                    if oc == 0:
                        self.dwait(t)
                    ps = self.PSA if par == 0 else self.PSB
                    for kc in range(DC):
                        mm = t.matmul(ps[:, :],
                                      Wdq[:, kc, 128 * oc:128 * (oc + 1)],
                                      HHc[:, kc, :128], start=(kc == 0),
                                      stop=(kc == DC - 1))
                    return mm

                def cq_cf(v, par, oc):
                    ps = self.PSA if par == 0 else self.PSB
                    return v.tensor_copy(cqT[:, oc, :], ps[:, :])
                self.pipe2([(lambda t, par, oc=oc: cq_pf(t, par, oc),
                             lambda v, par, oc=oc: cq_cf(v, par, oc))
                            for oc in range(CQC)])

                def qc_pf(t, par, h):
                    hb = (h % 2) * 64
                    for kc in range(CQC):
                        mm = t.matmul(self.PSA[hb:hb + 64, :],
                                      Wuq[:, kc, 64 * h:64 * (h + 1)],
                                      cqT[:, kc, :], start=(kc == 0),
                                      stop=(kc == CQC - 1))
                    return mm

                def qc_cf(v, par, h):
                    hb = (h % 2) * 64
                    return v.tensor_copy(qcT[hb:hb + 64, h // 2, :],
                                         self.PSA[hb:hb + 64, :])
                self.pipe2([(lambda t, par, h=h: qc_pf(t, par, h),
                             lambda v, par, h=h: qc_cf(v, par, h))
                            for h in range(H1)])

                def qm_pf(t, par, oc):
                    ps = self.PSA if par == 0 else self.PSB
                    for kc in range(CQC):
                        mm = t.matmul(ps[:, :], cqT[:, kc, :],
                                      Wuq[:, kc, 128 * oc:128 * (oc + 1)],
                                      start=(kc == 0), stop=(kc == CQC - 1))
                    return mm

                def qm_cf(v, par, oc):
                    ps = self.PSA if par == 0 else self.PSB
                    return v.tensor_copy(qctm[:, 128 * oc:128 * (oc + 1)],
                                         ps[:, :])
                self.pipe2([(lambda t, par, oc=oc: qm_pf(t, par, oc),
                             lambda v, par, oc=oc: qm_cf(v, par, oc))
                            for oc in range(H1 * DH1 // 128)])
                for oc in range(H1 * DR // 128):
                    self.blk(tensor=lambda e, oc=oc: [e.matmul(
                        self.PSA[:, :], cqT[:, kc, :],
                        Wqr[:, kc, 128 * oc:128 * (oc + 1)],
                        start=(kc == 0), stop=(kc == CQC - 1))
                        for kc in range(CQC)])
                    self.blk(scalar=lambda e, oc=oc: self.seq("act", e, [
                        lambda s: s.activation(self.SA[:, :128], self.PSA[:, :],
                                               AF.Abs),
                        lambda s: s.activation(self.SA[:, :128], self.SA[:, :128],
                                               AF.Exp, scale=-1.0),
                    ]))
                    self.blk(vector=lambda e: e.tensor_scalar_add(
                        self.SA[:, :128], self.SA[:, :128], 1.0))
                    self.blk(scalar=lambda e: e.activation(
                        self.SA[:, :128], self.SA[:, :128], AF.Ln))
                    self.blk(vector=lambda e, oc=oc: e.scalar_tensor_tensor(
                        ur[:, 128 * oc:128 * (oc + 1)], self.PSA[:, :], 0.0,
                        self.SA[:, :128], ALU.max, ALU.add))
                for ver in ("A", "B"):
                    QQ = qXT[ver]
                    self.blk(gpsimd=lambda e, ver=ver: self.dma(
                        "sw", e, QRC[:, :],
                        self.CN["qrot" + ver][tsl, :]))
                    cs_ap = bass.AP(QRC, 0, [[32, 128], [0, H1], [1, 16]])
                    sn_ap = bass.AP(QRC, 16, [[32, 128], [0, H1], [1, 16]])
                    rr = lambda t: t.rearrange("p (h x d) -> p h x d", h=H1, x=2)
                    u1, u2 = rr(ur[:])[:, :, 0, :], rr(ur[:])[:, :, 1, :]
                    S1 = self.SA[:, :H1 * 16].rearrange("p (h d) -> p h d", h=H1)
                    S2 = self.SB_[:, :H1 * 16].rearrange("p (h d) -> p h d", h=H1)
                    d1, d2 = rr(QTM[:])[:, :, 0, :], rr(QTM[:])[:, :, 1, :]
                    self.blk(vector=lambda e: self.dwait(e))
                    self.blk(vector=lambda e, u1=u1, u2=u2, S1=S1, S2=S2,
                             d1=d1, d2=d2, cs_ap=cs_ap, sn_ap=sn_ap:
                             self.seq("dve", e, [
                        lambda v: v.tensor_tensor(S1, u1, cs_ap, ALU.mult),
                        lambda v: v.tensor_tensor(S2, u2, sn_ap, ALU.mult),
                        lambda v: v.tensor_sub(S1, S1, S2),
                        lambda v: v.tensor_copy(d1, S1),
                        lambda v: v.tensor_tensor(S1, u1, sn_ap, ALU.mult),
                        lambda v: v.tensor_tensor(S2, u2, cs_ap, ALU.mult),
                        lambda v: v.tensor_add(S1, S1, S2),
                        lambda v: v.tensor_copy(d2, S1),
                        lambda v: v.tensor_copy(QB16[:, :], QTM[:, :]),
                    ]))
                    if ver == "A":
                        self.blk(vector=lambda e: e.tensor_copy(
                            QATMB[:, :], QB16[:, :]))

                    def rt_pf(t, par, h):
                        return t.transpose(
                            self.PSTB[64 * par:64 * par + 32, :],
                            QB16[:, 32 * h:32 * (h + 1)], self.IDB[:, :])

                    def rt_cf(v, par, h, QQ):
                        return v.tensor_copy(
                            QQ[:, h, :], self.PSTB[64 * par:64 * par + 32, :])
                    self.pipe2([(lambda t, par, h=h: rt_pf(t, par, h),
                                 lambda v, par, h=h, QQ=QQ: rt_cf(v, par, h,
                                                                  QQ))
                                for h in range(H1)])
                if dbg == 5:
                    return
                i1sl = IDX1[:, 256 * tt:256 * (tt + 1)]
                i2a = IDX2[:, 256 * tt:256 * tt + 128]
                i2b = IDX2[:, 256 * tt + 128:256 * (tt + 1)]

                for pr in range(H1 // 2):
                    def gkf(e, pr=pr):
                        for j in range(4):
                            self._scnt["sw"] += 16
                            e.dma_gather(
                                KFG[:, 1024 * j:1024 * (j + 1)].rearrange(
                                    "p (s d) -> p s d", d=128),
                                d_kf[pr],
                                IDX1[:, 256 * tt + 64 * j:
                                     256 * tt + 64 * (j + 1)],
                                1024, self.greg(e, 1024),
                                128, elem_step=128,
                            ).then_inc(self._sem["sw"], 16)
                    self.blk(gpsimd=gkf)

                    def vcont(e, pr=pr):
                        self.dwait(e)
                        fns = []
                        for j in range(2):
                            h_ = 2 * pr + j
                            fns += [
                                lambda v, j=j: v.tensor_tensor(
                                    EPROD[:, :2048],
                                    KFG[:].rearrange(
                                        "p (s d) -> p s d",
                                        d=128)[:, :, 64 * j:64 * (j + 1)],
                                    bass.AP(qctm, 128 * pr + 64 * j,
                                            [[H1 * DH1, 128], [0, 32],
                                             [1, 64]]), ALU.mult),
                                lambda v, h_=h_: v.tensor_reduce(
                                    CONT[:, 32 * h_:32 * (h_ + 1)],
                                    EPROD[:, :2048].rearrange(
                                        "p (r d) -> p r d", d=64),
                                    AX.X, ALU.add),
                            ]
                        self.seq("dve", e, fns)
                    self.blk(vector=vcont)
                if dbg in (6, 8, 9):
                    return
                for hf in range(2):
                    def gmr(e, hf=hf):
                        i2o = 256 * tt + 128 * hf
                        for j in range(2):
                            self._scnt["sw"] += 16
                            e.dma_gather(
                                MRG[:, 1024 * j:1024 * (j + 1)].rearrange(
                                    "p (s d) -> p s d", d=128),
                                (d_prA if hf == 0 else d_prB)[:, :],
                                IDX2[:, i2o + 64 * j:i2o + 64 * (j + 1)],
                                1024, self.greg(e, 1024), 128,
                                elem_step=128).then_inc(self._sem["sw"], 16)
                    self.blk(gpsimd=gmr)

                    def vrope(e, hf=hf):
                        self.dwait(e)
                        fns = []
                        for h in range(H1):
                            fns += [
                                lambda v, h=h: v.tensor_tensor(
                                    EPROD[:, 512 * hf:512 * (hf + 1)],
                                    MRG[:].rearrange("p (s d) -> p s d",
                                                     d=128)[:, :, 0:32],
                                    bass.AP(QATMB, 32 * h,
                                            [[H1 * 32, 128], [0, 16],
                                             [1, 32]]), ALU.mult),
                                lambda v, h=h: v.tensor_reduce(
                                    ROPE[:, 32 * h + 16 * hf:
                                         32 * h + 16 * (hf + 1)],
                                    EPROD[:, 512 * hf:512 * (hf + 1)]
                                    .rearrange("p (r d) -> p r d", d=32),
                                    AX.X, ALU.add),
                            ]
                        self.seq("dve", e, fns)
                    self.blk(vector=vrope)
                if dbg == 7:
                    return
                for h in range(H1):
                    hc, hr = h // hp1, (h % hp1) * DH1
                    def tband(e, h=h, hc=hc, hr=hr, w0=w0, wN=wN):
                        e.matmul(self.PS0[:, :wN], qcT[hr:hr + DH1, h // 2, :],
                                 KW[hr:hr + DH1, hc, :wN],
                                 start=True, stop=False)
                        e.matmul(self.PS0[:, :wN], qXT["B"][:, h, :],
                                 mB[:, w0:w0 + wN], start=False, stop=True)
                        if tt == 0:
                            e.matmul(self.PS1[:, :wN], qcT[hr:hr + DH1, h // 2, :],
                                     KW[hr:hr + DH1, hc, :wN],
                                     start=True, stop=False)
                            e.matmul(self.PS1[:, :wN], qXT["A"][:, h, :],
                                     mA[:, :wN], start=False, stop=True)
                    self.blk(tensor=tband)
                    MB_ = MBAND0 if tt == 0 else MBAND
                    wsl = slice(0, wN)
                    def vsm(e, h=h, wsl=wsl, wN=wN, par=par):
                        self.seq("dve", e, [
                            lambda v: v.tensor_copy(
                                ES[:, :], ROPE[:, 32 * h:32 * (h + 1)]),
                            lambda v: v.tensor_add(
                                ES[:, :], ES[:, :],
                                CONT[:, 32 * h:32 * (h + 1)]),
                            lambda v: v.tensor_add(
                                ES[:, :], ES[:, :],
                                EMK[:, 32 * tt:32 * (tt + 1)]),
                            lambda v: v.tensor_tensor(
                                self.SA[:, wsl], self.PS0[:, :wN],
                                MB_[:, wsl], ALU.add),
                        ] + ([
                            lambda v: v.tensor_tensor(
                                self.SB_[:, wsl], self.PS1[:, :wN],
                                MABS0[:, wsl], ALU.add),
                            lambda v: v.tensor_max(
                                self.SA[:, wsl], self.SA[:, wsl],
                                self.SB_[:, wsl]),
                        ] if tt == 0 else []) + [
                            lambda v: v.tensor_reduce(
                                rmax[:, 0:1], self.SA[:, wsl], AX.X, ALU.max),
                            lambda v: v.tensor_reduce(
                                rmax[:, 1:2], ES[:, :], AX.X, ALU.max),
                            lambda v: v.tensor_max(
                                rmax[:, 0:1], rmax[:, 0:1], rmax[:, 1:2]),
                            lambda v: v.tensor_scalar(
                                rmax[:, 2:3], rmax[:, 0:1], -scale1, None,
                                ALU.mult),
                        ])
                    self.blk(vector=vsm)
                    def sexp(e, h=h, wN=wN, par=par):
                        self.seq("act", e, [
                            lambda s: s.activation(
                                PRB[:, par, 224 * h:224 * h + wN],
                                self.SA[:, 0:wN], AF.Exp,
                                bias=rmax[:, 2:3], scale=scale1,
                                accum_out=zc[:, 0:1]),
                            lambda s: s.activation(
                                PRB[:, par, 224 * h + 192:224 * h + 224],
                                ES[:, :], AF.Exp,
                                bias=rmax[:, 2:3], scale=scale1,
                                accum_out=zc[:, 1:2]),
                        ])
                    self.blk(scalar=sexp)
                    self.blk(vector=lambda e, h=h, par=par: self.seq("dve", e, [
                        lambda v: v.tensor_add(zc[:, 0:1], zc[:, 0:1],
                                               zc[:, 1:2]),
                        lambda v: v.reciprocal(ZCOL[:, par, h:h + 1],
                                               zc[:, 0:1]),
                    ]))

            def pv_phase(tt):
                par = tt % 2
                tsl = slice(128 * tt, 128 * (tt + 1))
                Wd = 128 * (tt + 1)
                w0 = max(0, 128 * tt - 64)
                wN = min(192, Wd - w0)
                WdS = 128 if tt == 0 else 128 * tt + 64
                for h in range(H1):
                    def gsc(e, h=h, par=par, WdS=WdS):
                        self.lib(e, LC.local_scatter)
                        e.local_scatter(PD[:, :WdS],
                                        PRB[:, par, 224 * h + 192:224 * h + 224],
                                        SIGI[:, 32 * tt:32 * (tt + 1)],
                                        128, WdS, 32)
                    self.blk(gpsimd=gsc)
                    aN = WdS - w0 if tt > 0 else wN
                    self.blk(vector=lambda e, h=h, par=par, w0=w0, wN=wN,
                             aN=aN, WdS=WdS: self.seq("dve", e, [
                                 lambda v: v.tensor_add(
                                     PD[:, w0:w0 + aN], PD[:, w0:w0 + aN],
                                     PRB[:, par, 224 * h:224 * h + aN])] + ([
                                 lambda v: v.tensor_copy(
                                     PD[:, WdS:w0 + wN],
                                     PRB[:, par, 224 * h + aN:224 * h + wN])]
                                 if wN > aN else [])))
                    nkc = Wd // 128
                    evT = self.resv("pe", nkc)
                    evC = self.resv("dve", nkc)
                    evPV = self.resv("pe2", nkc)
                    PTb2 = [self.SAB16[:, 0:128], self.SAB16[:, 128:256]]

                    with self.nc.Block() as block:
                        @block.tensor
                        def _(t, nkc=nkc, evT=evT, evC=evC, evPV=evPV):
                            self.fire(t.transpose(
                                self.PSTB[:, :], PD[:, 0:128],
                                self.IDB[:, :]), evT[0])
                            for ci in range(nkc):
                                self.wait(t, evC[ci])
                                self.fire(t.matmul(
                                    self.PS0[:, :DCKV], PTb2[ci % 2],
                                    ckvtm[:, ci, :], start=(ci == 0),
                                    stop=(ci == nkc - 1)), evPV[ci])
                                if ci + 1 < nkc:
                                    self.fire(t.transpose(
                                        self.PSTB[:, :],
                                        PD[:, 128 * (ci + 1):128 * (ci + 2)],
                                        self.IDB[:, :]), evT[ci + 1])

                        @block.vector
                        def _(v, nkc=nkc, evT=evT, evC=evC, evPV=evPV):
                            for ci in range(nkc):
                                self.wait(v, evT[ci])
                                if ci >= 2:
                                    self.wait(v, evPV[ci - 2])
                                self.fire(v.tensor_copy(
                                    PTb2[ci % 2], self.PSTB[:, :]),
                                    evC[ci])
                    self.blk(vector=lambda e, h=h, par=par: e.tensor_scalar(
                        self.SA[:, :DCKV], self.PS0[:, :DCKV],
                        ZCOL[:, par, h:h + 1], None, ALU.mult))
                    def oc_pf(t, par, kvc):
                        pd = self.PSTF if par == 0 else self.PSA
                        return t.transpose(
                            pd[:, :], self.SA[:, 128 * kvc:128 * (kvc + 1)],
                            self.IDF[:, :])

                    def oc_cf(v, par, kvc):
                        pd = self.PSTF if par == 0 else self.PSA
                        return v.tensor_copy(OCT[:, kvc, :], pd[:, :])
                    self.pipe2([(lambda t, par, kvc=kvc: oc_pf(t, par, kvc),
                                 lambda v, par, kvc=kvc: oc_cf(v, par, kvc))
                                for kvc in range(KVC)])
                    def tuv(e, h=h):
                        for kvc in range(KVC):
                            e.matmul(self.PSD[(h % hp1) * DH1:
                                              (h % hp1) * DH1 + DH1, :],
                                     Wuv[:, kvc, DH1 * h:DH1 * (h + 1)],
                                     OCT[:, kvc, :], start=(kvc == 0),
                                     stop=(kvc == KVC - 1))
                    self.blk(tensor=tuv)
                    if h % hp1 == hp1 - 1:
                        self.blk(vector=lambda e, h=h: e.tensor_copy(
                            ATT[:, h // hp1, :], self.PSD[:, :]))
                for oc in range(DC):
                    self.blk(tensor=lambda e, oc=oc: [e.matmul(
                        self.PSA[:, :], WoutB[:, kc, 128 * oc:128 * (oc + 1)],
                        ATT[:, kc, :], start=(kc == 0), stop=(kc == KC - 1))
                        for kc in range(KC)])
                    self.blk(vector=lambda e, oc=oc: e.scalar_tensor_tensor(
                        self.X[:, oc, tsl], self.PSA[:, :],
                        self.MOD["ga"][:, oc:oc + 1], self.X[:, oc, tsl],
                        ALU.mult, ALU.add))

            for tp in range(0, NT, 2):
                score_phase(tp)
                if dbg == 4:
                    continue
                if tp + 1 < NT:
                    score_phase(tp + 1)
                self.blk(gpsimd=lambda e: self.lib(e, LC.local_scatter))
                pv_phase(tp)
                if tp + 1 < NT:
                    pv_phase(tp + 1)
                self.blk(gpsimd=lambda e: self.lib(e, LC.mlp))

    def stage_l1_ffn(self):
        T, D, DC, NT, E = self.T, self.D, self.DC, self.NT, self.E
        HID, RH = self.HID, self.RH
        TC4 = T // self.TCW
        W = self.TCW
        nks = HID // 128
        nkr = E * RH // 128
        nkt = nks + nkr
        with self.scope():
            d_gates = self.dint("d_gates", [E, T])
            d_w3bf = self.dint("d_w3bf", [HID + E * RH, D], BF16)
            CST = self.sb("CST", [128, D], BF16)

            def gcast(e):
                for k in range(nks):
                    self.dma("sw", e, CST[:, :],
                             self.i_sw3T[128 * k:128 * (k + 1), :])
                    self.dwait(e, ("sw",))
                    self.dma("sw", e, d_w3bf[128 * k:128 * (k + 1), :], CST[:, :])
                    self.dwait(e, ("sw",))
                for ex in range(E):
                    for k in range(RH // 128):
                        kk = nks + ex * (RH // 128) + k
                        self.dma("sw", e, CST[:, :],
                                 self.i_w3T[ex, 128 * k:128 * (k + 1), :])
                        self.dwait(e, ("sw",))
                        self.dma("sw", e, d_w3bf[128 * kk:128 * (kk + 1), :],
                                 CST[:, :])
                        self.dwait(e, ("sw",))
            self.blk(gpsimd=gcast)
            RW = self.sb("RW", [128, DC, E], F32)
            GT = self.sb("GT", [8, T], F32)
            onepc = self.sb("onepc2", [128, DC], F32)
            self.blk(gpsimd=lambda e: [self.dma(
                "sw", e, RW[:, kc, :], self.i_routerT[128 * kc:128 * (kc + 1), :])
                for kc in range(DC)])
            self.blk(vector=lambda e: (self.dwait(e), e.tensor_scalar_add(
                onepc[:], self.MOD["cb"][:], 1.0)))
            for tc in range(TC4):
                sl = slice(W * tc, W * (tc + 1))
                for kc in range(DC):
                    self.blk(vector=lambda e, sl=sl, kc=kc: self.seq("dve", e, [
                        lambda v: v.tensor_mul(self.SA[:, :W], self.X[:, kc, sl],
                                               self.ROWRK[:, sl]),
                        lambda v: v.tensor_sub(self.SA[:, :W], self.SA[:, :W],
                                               self.ROWWK[:, sl]),
                        lambda v: v.tensor_scalar(self.SA[:, :W], self.SA[:, :W],
                                                  onepc[:, kc:kc + 1],
                                                  self.MOD["sh"][:, kc:kc + 1],
                                                  ALU.mult, ALU.add),
                    ]))
                    self.blk(tensor=lambda e, kc=kc: e.matmul(
                        self.PS1[0:8, :W], RW[:, kc, :], self.SA[:, :W],
                        start=(kc == 0), stop=(kc == DC - 1)))
                self.blk(vector=lambda e, sl=sl: e.tensor_copy(
                    GT[0:8, sl], self.PS1[0:8, :W]))
            self.blk(scalar=lambda e: e.activation(
                GT[0:8, :], GT[0:8, :], AF.Sigmoid))
            SCT = self.sb("SCT", [128, 8], F32)
            u8 = self.sb("u8f", [128, 8], U32)
            i0c = self.sb("i0c", [128, 2], F32)
            IOT8 = self.sb("IOT8", [128, 8], F32)
            EB = self.sb("EB", [128, 8], F32)
            MSK = self.sb("MSK", [128, 8], BF16)
            MSKT = self.sb("MSKT", [8, T], F32)
            self.blk(gpsimd=lambda e: (
                self.lib(e, LC.standard),
                e.iota(IOT8[:, :], [[1, 8]], channel_multiplier=0,
                       allow_small_or_imprecise_dtypes=True),
                self.dma("sw", e, EB[:, :],
                         bass.AP(self.i_expbias.tensor, 0, [[0, 128], [1, 8]]))))
            self.blk(vector=lambda e: self.dwait(e))
            for tt in range(NT):
                tsl = slice(128 * tt, 128 * (tt + 1))
                self.blk(tensor=lambda e, tsl=tsl: e.transpose(
                    self.PSTF[:, 0:8], GT[0:8, tsl], self.IDF[0:8, 0:8]))
                self.blk(vector=lambda e: e.tensor_tensor(
                    SCT[:, :], self.PSTF[:, 0:8], EB[:, :], ALU.add))
                self.blk(vector=lambda e: self.seq("dve", e, [
                    lambda v: v.max(self.SA[:, 0:8], SCT[:, :]),
                    lambda v: v.max_index(u8[:, :], self.SA[:, 0:8], SCT[:, :]),
                    lambda v: v.tensor_copy(i0c[:, :], u8[:, 0:2]),
                    lambda v: v.tensor_scalar(MSK[:, :], IOT8[:, :],
                                              i0c[:, 0:1], None, ALU.is_equal),
                    lambda v: v.tensor_scalar(self.SB_[:, 0:8], IOT8[:, :],
                                              i0c[:, 1:2], None, ALU.is_equal),
                    lambda v: v.tensor_add(MSK[:, :], MSK[:, :],
                                           self.SB_[:, 0:8]),
                ]))
                self.blk(tensor=lambda e, tt=tt: e.transpose(
                    self.PSTB[0:8, :], MSK[:, :], self.IDB[:, :]))
                self.blk(vector=lambda e, tsl=tsl: e.tensor_copy(
                    MSKT[0:8, tsl], self.PSTB[0:8, :]))
            self.blk(vector=lambda e: e.tensor_mul(GT[0:8, :], GT[0:8, :],
                                                   MSKT[0:8, :]))
            self.blk(sync=lambda e: self.dma("hw", e, d_gates[0:8, :],
                                             GT[0:8, :]))
            self.blk(vector=lambda e: self.dwait(e, ("hw",)))

            self.blk(sync=lambda e: [self.dma(
                "hw", e, self.d_xs[128 * c:128 * (c + 1), :], self.X[:, c, :])
                for c in range(DC)])
            self.blk(vector=lambda e: self.dwait(e, ("hw",)))

            HHr = self.sb("HHr", [128, DC, T], BF16)
            self.blk(gpsimd=lambda e: [self.dma(
                "sw", e, HHr[:, kc, :], self.d_hh[128 * kc:128 * (kc + 1), :])
                for kc in range(DC)])
            W1B = self.sb("W1B", [128, DC, 128], BF16)
            W2B = self.sb("W2B", [128, DC, 128], BF16)
            GATE = self.sb("GATE", [128, T], BF16)
            HP = self.sb("HP", [128, 2, T], BF16)
            self.blk(vector=lambda e: self.dwait(e))
            hwm = [0, 0]

            def pass1(w1_ap, c1off, w2_ap, c2off, tgt_ap, nhc, gated):
                for hc in range(nhc):
                    par = hc % 2
                    self.blk(gpsimd=lambda e, hc=hc: (
                        self.wload(e, W1B, w1_ap, c1off + 128 * hc, 128, DC),
                        self.wload(e, W2B, w2_ap, c2off + 128 * hc, 128, DC)))
                    evM = self.resv("pe", TC4)
                    evS = self.resv("act", TC4)
                    nv = 2 if gated else 1
                    evV = self.resv("dve", nv * TC4)

                    with self.nc.Block() as block:
                        @block.tensor
                        def _(t, evM=evM, evV=evV):
                            for tc in range(TC4):
                                sl = slice(W * tc, W * (tc + 1))
                                if tc == 0:
                                    self.dwait(t)
                                else:
                                    self.wait(t, evV[nv * tc - 1])
                                for kc in range(DC):
                                    t.matmul(self.PS0[:, :W], W1B[:, kc, :],
                                             HHr[:, kc, sl], start=(kc == 0),
                                             stop=(kc == DC - 1))
                                for kc in range(DC):
                                    mm = t.matmul(self.PS1[:, :W],
                                                  W2B[:, kc, :],
                                                  HHr[:, kc, sl],
                                                  start=(kc == 0),
                                                  stop=(kc == DC - 1))
                                self.fire(mm, evM[tc])

                        @block.scalar
                        def _(s, evM=evM, evS=evS, evV=evV):
                            for tc in range(TC4):
                                self.wait(s, evM[tc])
                                if tc >= 1:
                                    self.wait(s, evV[nv * tc - 1])
                                self.fire(s.activation(
                                    self.SA[:, :W], self.PS0[:, :W],
                                    AF.Silu), evS[tc])

                        @block.vector
                        def _(v, par=par, evS=evS, evV=evV):
                            for tc in range(TC4):
                                sl = slice(W * tc, W * (tc + 1))
                                self.wait(v, evS[tc])
                                if tc == 0 and hwm[par]:
                                    self.dwait(v, ("hw",))
                                if gated:
                                    self.fire(v.tensor_tensor(
                                        self.SB_[:, :W], self.SA[:, :W],
                                        self.PS1[:, :W], ALU.mult),
                                        evV[2 * tc])
                                    self.wait(v, evV[2 * tc])
                                    self.fire(v.tensor_mul(
                                        HP[:, par, sl], self.SB_[:, :W],
                                        GATE[:, sl]), evV[2 * tc + 1])
                                else:
                                    self.fire(v.tensor_tensor(
                                        HP[:, par, sl], self.SA[:, :W],
                                        self.PS1[:, :W], ALU.mult),
                                        evV[tc])
                    self.blk(sync=lambda e, hc=hc, par=par: self.dma(
                        "hw", e,
                        bass.AP(tgt_ap[0], tgt_ap[1] + 128 * hc * T,
                                [[T, 128], [1, T]]),
                        HP[:, par, :]))
                    hwm[par] = self._scnt["hw"]
            pass1(self.i_sw1T, 0, self.i_sw2T, 0, (self.d_g.tensor, 0),
                  nks, False)
            for ex in range(E):
                self.blk(gpsimd=lambda e, ex=ex: self.dma(
                    "sw", e, GATE[:, :],
                    bass.AP(d_gates.tensor, ex * T, [[0, 128], [1, T]])))
                self.blk(vector=lambda e: self.dwait(e))
                pass1(self.i_w12T[ex], 0, self.i_w12T[ex], RH,
                      (self.d_ge.tensor, ex * RH * T), RH // 128, True)
            self.blk(vector=lambda e: self.dwait(e, ("hw",)))

            KCH = 24
            nch = (nkt + KCH - 1) // KCH
            XTD = self.sb("XTD", [128, DC, W], F32)
            PANC = self.sb("PANC", [128, KCH, W], BF16)
            W3CC = self.sb("W3CC", [128, KCH, 128], BF16)
            for tc in range(TC4):
                sl = slice(W * tc, W * (tc + 1))
                self.blk(sync=lambda e, sl=sl: [self.dma(
                    "hw", e, XTD[:, dc, :],
                    self.d_xs[128 * dc:128 * (dc + 1), sl])
                    for dc in range(DC)])
                self.blk(vector=lambda e: self.dwait(e, ("hw",)))
                for ch in range(nch):
                    k0 = KCH * ch
                    kn = min(KCH, nkt - k0)

                    def ldp(e, k0=k0, kn=kn, sl=sl):
                        for j in range(kn):
                            k = k0 + j
                            if k < nks:
                                src = bass.AP(self.d_g.tensor,
                                              128 * k * T + sl.start,
                                              [[T, 128], [1, W]])
                            else:
                                src = bass.AP(self.d_ge.tensor,
                                              128 * (k - nks) * T + sl.start,
                                              [[T, 128], [1, W]])
                            self.dma("sw", e, PANC[:, j, :], src)
                    self.blk(gpsimd=ldp)
                    for dc in range(DC):
                        self.blk(gpsimd=lambda e, k0=k0, kn=kn, dc=dc: [
                            self.dma("sw", e, W3CC[:, j, :],
                                     d_w3bf[128 * (k0 + j):128 * (k0 + j + 1),
                                            128 * dc:128 * (dc + 1)])
                            for j in range(kn)])

                        def t2(e, kn=kn):
                            self.dwait(e)
                            for j in range(kn):
                                e.matmul(self.PS0[:, :W], W3CC[:, j, :],
                                         PANC[:, j, :], start=(j == 0),
                                         stop=(j == kn - 1))
                        self.blk(tensor=t2)
                        self.blk(vector=lambda e, dc=dc: e.scalar_tensor_tensor(
                            XTD[:, dc, :], self.PS0[:, :W],
                            self.MOD["gb"][:, dc:dc + 1],
                            XTD[:, dc, :], ALU.mult, ALU.add))
                self.blk(sync=lambda e, sl=sl: [self.dma(
                    "hw", e, self.o_outT[128 * dc:128 * (dc + 1), sl],
                    XTD[:, dc, :]) for dc in range(DC)])
                self.blk(vector=lambda e: self.dwait(e, ("hw",)))

    def stage_out(self):
        def g(e):
            for c in range(self.DC):
                self.dma("hw", e, self.o_outT[128 * c:128 * (c + 1), :],
                         self.X[:, c, :])
            self.dwait(e, ("hw",))
        self.blk(sync=g)

    def finish(self):
        self.es.close()
        self.nc.finalize()
        mybir.codegen_inst_isa_subclasses(self.nc)
        return self.nc


def _prep_core_inputs(cfg, inp, b):
    f32 = np.float32
    m = {}
    m["xT"] = np.ascontiguousarray(np.asarray(inp["L0_attn_x"][b], f32).T)
    m["wqkvT"] = np.ascontiguousarray(np.asarray(inp["L0_attn_w_qkv"], f32).T)
    m["wprojT"] = np.ascontiguousarray(np.asarray(inp["L0_attn_w_proj"], f32).T)
    m["Wr"] = np.asarray(inp["L0_ffn_Wr"], f32)
    m["br"] = np.asarray(inp["L0_ffn_br"], f32)[None, :]
    m["W1"] = np.asarray(inp["L0_ffn_W1"], f32)
    m["b1"] = np.asarray(inp["L0_ffn_b1"], f32)
    m["W2"] = np.asarray(inp["L0_ffn_W2"], f32)
    m["b2"] = np.asarray(inp["L0_ffn_b2"], f32)
    m["c0"] = np.asarray(inp["L0_c"][b], f32)[None, :]
    m["wmod0"] = np.asarray(inp["L0_wmod"], f32)
    m["Wdkv"] = np.asarray(inp["L1_attn_W_dkv"], f32)
    m["Wuk"] = np.asarray(inp["L1_attn_W_uk"], f32)
    m["WukT"] = np.ascontiguousarray(np.asarray(inp["L1_attn_W_uk"], f32).T)
    m["Wuv"] = np.asarray(inp["L1_attn_W_uv"], f32)
    m["Wdq"] = np.asarray(inp["L1_attn_W_dq"], f32)
    m["Wuq"] = np.asarray(inp["L1_attn_W_uq"], f32)
    m["Wqr"] = np.asarray(inp["L1_attn_W_qr"], f32)
    m["Wkr"] = np.asarray(inp["L1_attn_W_kr"], f32)
    m["Wout"] = np.asarray(inp["L1_attn_W_out"], f32)
    m["idxwq"] = np.asarray(inp["L1_attn_idx_wq"], f32)
    m["idxwk"] = np.asarray(inp["L1_attn_idx_wk"], f32)
    m["idxw"] = np.asarray(inp["L1_attn_idx_w"], f32)[None, :]
    m["w12T"] = np.ascontiguousarray(
        np.asarray(inp["L1_ffn_w12"], f32).transpose(0, 2, 1))
    m["w3T"] = np.ascontiguousarray(
        np.asarray(inp["L1_ffn_w3"], f32).transpose(0, 2, 1))
    m["sw1T"] = np.ascontiguousarray(np.asarray(inp["L1_ffn_shared_w1"], f32).T)
    m["sw2T"] = np.ascontiguousarray(np.asarray(inp["L1_ffn_shared_w2"], f32).T)
    m["sw3T"] = np.ascontiguousarray(np.asarray(inp["L1_ffn_shared_w3"], f32).T)
    m["routerT"] = np.ascontiguousarray(np.asarray(inp["L1_ffn_router_w"], f32).T)
    m["expbias"] = np.asarray(inp["L1_ffn_expert_bias"], f32)[None, :]
    m["c1"] = np.asarray(inp["L1_c"][b], f32)[None, :]
    m["wmod1"] = np.asarray(inp["L1_wmod"], f32)
    for k, v in cfg["consts"].items():
        m["cn_" + k] = np.asarray(v, f32)
    return m


def build_full(cfg):
    mk = MK(cfg)
    mk.declare_io()
    mk.stage_load()
    mk.stage_mod(mk.i_c0, mk.i_wmod0)
    mk.stage_ln("ca", "sa")
    mk.stage_l0_attn()
    mk.stage_ln("cb", "sh")
    mk.stage_l0_ffn()
    mk.stage_mod(mk.i_c1, mk.i_wmod1)
    mk.stage_ln("ca", "sa")
    mk.stage_l1_attn()
    mk.stage_ln("cb", "sh", keep_rows=True)
    mk.stage_l1_ffn()
    return mk


def kernel(**inputs):
    B, T, D = inputs["L0_attn_x"].shape
    cfg = dict(T=T, D=D, H0=16, DCKV=256, DCQ=512, H1=8, DH1=64,
               HID=4096, RH=1024)
    cfg["consts"] = make_consts(T)
    mk = build_full(cfg)
    nc = mk.finish()
    in_maps = [_prep_core_inputs(cfg, inputs, b) for b in range(B)]
    res = run_bass_kernel_spmd(nc, in_maps, core_ids=list(range(B)))
    out = np.stack([np.asarray(r["outT"]).T for r in res.results], 0)
    return np.ascontiguousarray(out.astype(np.float32))
